# version 01
# baseline (speedup 1.0000x reference)
import numpy as np
import concourse.bass as bass
import concourse.mybir as mybir
from concourse.bass_utils import run_bass_kernel_spmd

F32 = mybir.dt.float32
BF16 = mybir.dt.bfloat16
AF = mybir.ActivationFunctionType
ALU = mybir.AluOpType
AX = mybir.AxisListType

D = 1024
S = 4096
T = 2048
NT = 16
EXPERTS = 8
CAP = 2048
CAPL = 384
SLOTS = EXPERTS * CAPL
EPS = 1e-5

QUEUES = ("tensor", "vector", "scalar", "gpsimd", "sync")


class Prog:

    KD = 6

    def __init__(self, nc, auto_bar=False, sim_safe=True):
        self.nc = nc
        self.sim_safe = sim_safe
        self.sems = {}
        self.count = {q: 0 for q in QUEUES}
        self.dcount = {q: 0 for q in QUEUES}
        self.items = {q: [] for q in QUEUES}
        self.waited = {q: {} for q in QUEUES}
        self.n_instr = 0
        self.auto_bar = auto_bar

    def ev(self, q, dma=True):
        evs = [("c", q, self.count[q])]
        if dma:
            evs += self.dma_evs(q)
        return ("multi", evs)

    def dma_evs(self, q):
        out = []
        for k in range(self.KD):
            n = (self.dcount[q] + self.KD - 1 - k) // self.KD
            if n > 0:
                out.append(("d", q, k, 16 * n))
        return out

    def _expand(self, after):
        flat = []
        for ev in after:
            if ev is None:
                continue
            if ev[0] == "multi":
                flat.extend(ev[1])
            else:
                flat.append(ev)
        return flat

    def _resolve(self, q, after):
        waits = []
        for ev in self._expand(after):
            if ev[0] == "c":
                _, q2, v = ev
                if v <= 0 or (q2 == q and not self.sim_safe):
                    continue
                key = ("c", q2)
            else:
                _, q2, k, v = ev
                if v <= 0:
                    continue
                key = ("d", q2, k)
            if v > self.waited[q].get(key, 0):
                self.waited[q][key] = v
                waits.append((key, v))
        return waits

    def op(self, q, fn, after=(), inc=1):
        waits = self._resolve(q, after)
        self.items[q].append((fn, waits, ("c", q), inc))
        self.count[q] += inc
        self.n_instr += 1 + len(waits)
        return ("c", q, self.count[q])

    def dma(self, q, out, in_, after=(), slow=False):
        i = self.dcount[q]
        k = i % self.KD
        key = ("d", q, k)
        prior = 16 * (i // self.KD)
        waits = self._resolve(q, list(after) + [("d", q, k, prior)])
        if prior > 0 and (key, prior) not in waits                 and self.waited[q].get(key, 0) >= prior:
            pass
        def fn(e):
            return e.dma_start(out=out, in_=in_,
                               allow_slow_non_contiguous=slow)
        self.items[q].append((fn, waits, key, 16))
        self.dcount[q] += 1
        self.n_instr += 1 + len(waits)
        return ("d", q, k, prior + 16)

    def barrier(self):
        evs = []
        for q in QUEUES:
            evs.append(("c", q, self.count[q]))
            evs.extend(self.dma_evs(q))
        for q in QUEUES:
            waits = self._resolve(q, evs)
            if waits:
                self.items[q].append((None, waits, ("c", q), 1))
                self.count[q] += 1
                self.n_instr += 1 + len(waits)
        return [self.ev(q) for q in QUEUES]

    def mbar(self):
        if self.auto_bar:
            self.barrier()

    KR = 4

    def make_sems(self, nc, stack):
        for q in QUEUES:
            for r in range(self.KR):
                self.sems[("c", q, r)] = stack.enter_context(
                    nc.semaphore(f"s_{q}_{r}"))
            for k in range(self.KD):
                self.sems[("d", q, k)] = stack.enter_context(
                    nc.semaphore(f"s_{q}_d{k}"))

    def emit(self, block):
        waited = {}
        for q in QUEUES:
            for (_fn, waits, _k, _i) in self.items[q]:
                for (key, v) in waits:
                    waited.setdefault(key, set()).add(v)
        thr = {q: sorted(waited.get(("c", q), set())) for q in QUEUES}
        tmap = {q: {v: m for m, v in enumerate(thr[q])} for q in QUEUES}

        def cwait(e, q2, v):
            m = tmap[q2][v]
            e.wait_ge(self.sems[("c", q2, m % self.KR)], m // self.KR + 1)

        def make_body(q, lst):
            comp_idx = [i for i, it in enumerate(lst) if it[2] == ("c", q)]
            last_comp = comp_idx[-1] if comp_idx else -1
            wset = set(thr[q])

            def body(e):
                running = 0
                for i, (fn, waits, inckey, inc) in enumerate(lst):
                    for (key, v) in waits:
                        if key[0] == "c":
                            cwait(e, key[1], v)
                        else:
                            e.wait_ge(self.sems[key], v)
                    if inckey == ("c", q):
                        running += inc
                        ins = e.nop() if fn is None else fn(e)
                        if running in wset:
                            m = tmap[q][running]
                            ins.then_inc(self.sems[("c", q, m % self.KR)], 1)
                    else:
                        fn(e).then_inc(self.sems[inckey], inc)
            return body
        for q in QUEUES:
            getattr(block, q)(make_body(q, self.items[q]))


class Arena:

    def __init__(self, nc, name, cols_bf16):
        self.t = nc.alloc_sbuf_tensor(name, [128, cols_bf16], BF16)
        self.free_list = [(0, cols_bf16)]
        self.live = {}

    def alloc(self, name, cols, dtype):
        need = cols * (2 if dtype == F32 else 1)
        need = (need + 15) & ~15
        for i, (off, sz) in enumerate(self.free_list):
            if sz >= need:
                if sz == need:
                    del self.free_list[i]
                else:
                    self.free_list[i] = (off + need, sz - need)
                self.live[name] = (off, need)
                ap = self.t.ap()[:, off:off + need]
                if dtype == F32:
                    ap = ap.bitcast(F32)[:, 0:cols]
                else:
                    ap = ap[:, 0:cols]
                return ap
        raise MemoryError(f"arena: cannot alloc {name} ({need} cols); "
                          f"free={self.free_list} live={sorted(self.live)}")

    def sub(self, parent, bf16_off, cols, dtype):
        off, sz = self.live[parent]
        need = cols * (2 if dtype == F32 else 1)
        assert bf16_off + need <= sz, (parent, bf16_off, need, sz)
        ap = self.t.ap()[:, off + bf16_off:off + bf16_off + need]
        if dtype == F32:
            ap = ap.bitcast(F32)[:, 0:cols]
        else:
            ap = ap[:, 0:cols]
        return ap

    def free(self, name):
        off, sz = self.live.pop(name)
        self.free_list.append((off, sz))
        self.free_list.sort()
        merged = []
        for off, sz in self.free_list:
            if merged and merged[-1][0] + merged[-1][1] == off:
                merged[-1] = (merged[-1][0], merged[-1][1] + sz)
            else:
                merged.append((off, sz))
        self.free_list = merged


def build_program(dbg=(), phase_limit=99, auto_bar=False, p1_windows=12,
                  sim_safe=True, p1_mode=0):
    nc = bass.Bass("TRN2", target_bir_lowering=False)

    def din(name, shape, dtype=F32):
        return nc.dram_tensor(name, list(shape), dtype, kind="ExternalInput").ap()

    x_full_h = din("x_full", (S, D))
    x_own_h = din("x_own", (T, D))
    Mq_h = din("Mq", (D, 512)); Mk_h = din("Mk", (D, 512)); Mv_h = din("Mv", (D, D))
    G1R_h = din("G1R", (32, 4, 32)); G2R_h = din("G2R", (4, 16, 16))
    G1I_h = din("G1I", (32, 4, 32)); G2I_h = din("G2I", (4, 16, 16))
    bR_h = din("bR", (512,)); bI_h = din("bI", (512,))
    Win_h = din("W_in", (D, 4096)); convw_h = din("conv_w", (2048, 4))
    lng_h = din("ln_g", (2048,)); lnbh_h = din("ln_b", (2048,))
    Wxp_h = din("W_xp", (2048, 33)); Ap_h = din("A_p", (2048, 16))
    Dv_h = din("Dvec", (2048,)); Wout_h = din("W_out", (2048, D))
    wg_h = din("wg", (D, 8))
    w1_h = din("w1", (EXPERTS, D, 4096)); b1_h = din("b1", (EXPERTS, 4096))
    w2_h = din("w2", (EXPERTS, 4096, D)); b2_h = din("b2", (EXPERTS, D))
    ranksel_h = din("rank_sel", (8, 1))
    halosel_h = din("halo_sel24", (24, 3))

    out_h = nc.dram_tensor("out", [T, D], F32, kind="ExternalOutput").ap()

    dbg_outs = {}

    def dout(name, shape, dtype=F32):
        if name not in dbg_outs:
            dbg_outs[name] = nc.dram_tensor("dbg_" + name, list(shape), dtype,
                                            kind="ExternalOutput").ap()
        return dbg_outs[name]

    cc_halo_in = nc.dram_tensor("cc_halo_in", [3, D], F32, kind="Internal",
                                addr_space="Local")
    cc_halo_out = nc.dram_tensor("cc_halo_out", [24, D], F32, kind="Internal",
                                 addr_space="Shared")
    cc_cnt_in = nc.dram_tensor("cc_cnt_in", [8, 1], F32, kind="Internal",
                               addr_space="Local")
    cc_cnt_out = nc.dram_tensor("cc_cnt_out", [64, 1], F32, kind="Internal",
                                addr_space="Shared")

    c_identf = nc.inline_tensor(np.eye(128, dtype=np.float32), name="c_identf")
    c_ones1 = nc.inline_tensor(np.ones((1, 128), dtype=np.float32), name="c_ones1")
    c_iota512 = nc.inline_tensor(
        np.broadcast_to(np.arange(512, dtype=np.float32), (128, 512)).copy(),
        name="c_iota512")
    c_ones8 = nc.inline_tensor(np.ones((8, 1), dtype=np.float32), name="c_ones8")
    c_lnscale = nc.inline_tensor(
        np.full((128, 1), 1.0 / 2048.0, dtype=np.float32), name="c_lnscale")
    c_ebase = nc.inline_tensor(
        (np.arange(8, dtype=np.float32) * CAPL - 1.0).reshape(8, 1).copy(),
        name="c_ebase")
    c_slotid = nc.inline_tensor(
        (np.arange(128, dtype=np.float32)[:, None]
         + 128.0 * np.arange(24, dtype=np.float32)[None, :]).copy(),
        name="c_slotid")
    seln = np.zeros((16, 16 * 128), dtype=np.float32)
    for n in range(16):
        seln[n, n * 128:(n + 1) * 128] = 1.0
    c_sel16 = nc.inline_tensor(seln, name="c_sel16")

    arena = Arena(nc, "arena", ((nc.sbuf_bytes_remaining - 512) // 2) & ~15)
    P = Prog(nc, auto_bar=auto_bar, sim_safe=sim_safe)

    from contextlib import ExitStack
    stack = ExitStack()
    with (
        nc.psum_tensor("psA", [128, 2048], F32) as psA,
        nc.psum_tensor("psB", [128, 2048], F32) as psB,
        stack,
        nc.Block() as block,
    ):
        P.make_sems(nc, stack)

        sb_identf = arena.alloc("identf", 128, F32)
        sb_identb = arena.alloc("identb", 128, BF16)
        sb_ones1 = arena.alloc("ones1", 128, F32)
        sb_ones1b = arena.alloc("ones1b", 128, BF16)
        sb_iota512 = arena.alloc("iota512", 512, F32)
        sb_ones8 = arena.alloc("ones8", 1, F32)
        sb_lnsc = arena.alloc("lnsc", 1, BF16)
        sb_ebase = arena.alloc("ebase", 1, F32)
        sb_slotid = arena.alloc("slotid", 24, F32)
        sb_ranksel = arena.alloc("ranksel", 1, F32)
        sb_halosel = arena.alloc("halosel", 3, BF16)

        P.dma("sync", sb_identf, c_identf.ap())
        P.dma("gpsimd", sb_identb, c_identf.ap())
        P.dma("sync", sb_ones1[0:1, :], c_ones1.ap())
        P.dma("gpsimd", sb_ones1b[0:1, :], c_ones1.ap())
        P.dma("sync", sb_iota512, c_iota512.ap())
        P.dma("sync", sb_ones8[0:8, :], c_ones8.ap())
        P.dma("gpsimd", sb_lnsc, c_lnscale.ap())
        P.dma("sync", sb_ebase[0:8, :], c_ebase.ap())
        P.dma("sync", sb_slotid, c_slotid.ap())
        P.dma("sync", sb_ranksel[0:8, :], ranksel_h)
        P.dma("gpsimd", sb_halosel[0:24, :], halosel_h)

        sb_stream = arena.alloc("stream", NT * 1024, F32)
        sb_eps = arena.alloc("eps", 1, F32)

        P.op("vector", lambda e: e.memset(sb_eps, EPS))

        def mm(ps_ap, lhsT, rhs, start, stop, after=()):
            return P.op("tensor", lambda e: e.matmul(
                ps_ap, lhsT, rhs, start=start, stop=stop), after=after)

        def tpose(ps_ap, in_ap, after=(), f32=True):
            p = in_ap.shape[0]
            ident = sb_identf[0:p, 0:p]
            return P.op("tensor", lambda e: e.transpose(ps_ap, in_ap, ident),
                        after=after)

        def act(q, out, in_, func, bias=0.0, scale=1.0, accum=None, after=()):
            return P.op(q, lambda e: e.activation(
                out, in_, func, bias=bias, scale=scale, accum_out=accum),
                after=after)

        def vcopy(out, in_, after=()):
            return P.op("vector", lambda e: e.tensor_copy(out, in_), after=after)

        def scopy(out, in_, after=()):
            return P.op("scalar", lambda e: e.copy(out, in_), after=after)

        def vtt(out, a, b, alu, after=()):
            return P.op("vector", lambda e: e.tensor_tensor(out, a, b, alu),
                        after=after)

        def vadd(out, a, b, after=()):
            return P.op("vector", lambda e: e.tensor_add(out, a, b), after=after)

        def vts(out, in0, s1, s2, op0, op1=None, after=()):
            if op1 is None:
                return P.op("vector", lambda e: e.tensor_scalar(
                    out, in0, s1, None, op0=op0), after=after)
            return P.op("vector", lambda e: e.tensor_scalar(
                out, in0, s1, s2, op0=op0, op1=op1), after=after)

        def vstt(out, in0, scalar, in1, op0, op1, after=()):
            return P.op("vector", lambda e: e.scalar_tensor_tensor(
                out, in0, scalar, in1, op0=op0, op1=op1), after=after)

        def ln_tile(x_ap, out_ap, stat, after=()):
            n = x_ap.shape[1]
            p = x_ap.shape[0]
            st = stat[0:p, :]
            e1 = P.op("vector", lambda e: e.bn_stats(st[:, 0:6], x_ap[:, 0:n // 2]),
                      after=after)
            e1b = P.op("vector", lambda e: e.bn_stats(st[:, 6:12], x_ap[:, n // 2:n]),
                       after=[e1])
            e2 = P.op("vector", lambda e: e.bn_aggr(st[:, 12:14], st[:, 0:12]),
                      after=[e1b])
            e3 = act("scalar", st[:, 14:15], st[:, 13:14], AF.Ln,
                     bias=sb_eps[0:p, 0:1], after=[e2])
            e4 = act("scalar", st[:, 14:15], st[:, 14:15], AF.Exp, scale=-0.5,
                     after=[e3])
            e5 = vts(out_ap, x_ap, st[:, 12:13], st[:, 14:15],
                     ALU.subtract, ALU.mult, after=[e4])
            return e5

        def softplus(out_ap, in_ap, tmp_ap, after=()):
            e1 = act("scalar", tmp_ap, in_ap, AF.Abs, after=after)
            e2 = act("scalar", tmp_ap, tmp_ap, AF.Exp, scale=-1.0, after=[e1])
            e3 = act("scalar", tmp_ap, tmp_ap, AF.Ln, bias=1.0, after=[e2])
            e4 = act("scalar", out_ap, in_ap, AF.Relu, after=[e3])
            e5 = vadd(out_ap, out_ap, tmp_ap, after=[e4])
            return e5

        tp_user = {0: None, 1: None}

        def tp_tile_to_fmajor(src_tile, dst, dst_cols, j, prev_evs, f32=False):
            stride = dst_cols(1) - dst_cols(0)
            evs = []
            for g in range(2):
                sl = g % 2
                base = 1024 + sl * 512
                dep = list(prev_evs)
                if tp_user[sl] is not None:
                    dep.append(tp_user[sl])
                ets = []
                for k in range(4):
                    fc = g * 4 + k
                    ets.append(tpose(psA[:, base + k * 128: base + k * 128 + 128],
                                     src_tile[:, fc * 128:(fc + 1) * 128],
                                     after=dep if k == 0 else [], f32=f32))
                dview = dst[:, dst_cols(g * 4) + j * 128:
                            dst_cols(g * 4) + j * 128 + 3 * stride + 128]                     .rearrange("p (c x) -> p c x", c=1)
                dview = bass.AP(dview.tensor, dview.offset,
                                [list(dview.ap[0]), [stride, 4], [1, 128]])
                ee = P.op("scalar", lambda e, dview=dview, base=base: e.copy(
                    dview, psA[:, base:base + 512]
                    .rearrange("p (c x) -> p c x", x=128)), after=[ets[-1]])
                tp_user[sl] = ee
                evs.append(ee)
            return evs

        sb_Mq = arena.alloc("Mq", 8 * 512, BF16)
        sb_Mk = arena.alloc("Mk", 8 * 512, BF16)
        sb_Mv = arena.alloc("Mv", 8 * 1024, BF16)
        for c in range(8):
            P.dma("gpsimd", sb_Mq[:, c * 512:(c + 1) * 512],
                  Mq_h[c * 128:(c + 1) * 128, :])
            P.dma("gpsimd", sb_Mk[:, c * 512:(c + 1) * 512],
                  Mk_h[c * 128:(c + 1) * 128, :])
            P.dma("gpsimd", sb_Mv[:, c * 1024:(c + 1) * 1024],
                  Mv_h[c * 128:(c + 1) * 128, :])

        sb_convw = arena.alloc("convw", 64, F32)
        P.dma("sync", sb_convw.rearrange("p (c j) -> p c j", j=4),
              convw_h.rearrange("(c p) j -> p c j", p=128), slow=True)
        sb_lng = arena.alloc("lng", 16, F32)
        sb_lnbeta = arena.alloc("lnbeta", 16, F32)
        P.dma("sync", sb_lng, lng_h.rearrange("(c p) -> p c", p=128), slow=True)
        P.dma("sync", sb_lnbeta, lnbh_h.rearrange("(c p) -> p c", p=128), slow=True)
        sb_Dv = arena.alloc("Dv", 16, F32)
        P.dma("sync", sb_Dv, Dv_h.rearrange("(c p) -> p c", p=128), slow=True)
        sb_Wxp = arena.alloc("Wxp", 16 * 33, BF16)
        for c in range(16):
            P.dma("gpsimd", sb_Wxp[:, c * 33:(c + 1) * 33],
                  Wxp_h[c * 128:(c + 1) * 128, :])
        sb_wg = arena.alloc("wg", 8 * 8, F32)
        for c in range(8):
            P.dma("sync", sb_wg[:, c * 8:(c + 1) * 8],
                  wg_h[c * 128:(c + 1) * 128, :])
        sb_A = arena.alloc("A", 256, F32)
        sb_Atmp = arena.sub("stream", 11264, 256, F32)
        evA = P.dma("sync", sb_A.rearrange("p (c n) -> p c n", n=16),
                    Ap_h.rearrange("(c p) n -> p c n", p=128), slow=True)
        evA = softplus(sb_A, sb_A, sb_Atmp, after=[evA])
        evA = vts(sb_A, sb_A, 0.1, 10.0, ALU.max, ALU.min, after=[evA])
        evA = vts(sb_A, sb_A, -1.0, None, ALU.mult, after=[evA])
        sb_b1 = arena.alloc("b1", 8 * 32, F32)
        for e_ in range(EXPERTS):
            P.dma("sync", sb_b1[:, e_ * 32:(e_ + 1) * 32],
                  b1_h[e_].rearrange("(c p) -> p c", p=128), slow=True)
        sb_b2 = arena.alloc("b2", 1024, BF16)
        for e_ in range(EXPERTS):
            P.dma("gpsimd", sb_b2[e_:e_ + 1, 0:1024],
                  b2_h[e_].rearrange("(o m) -> o m", o=1))
        ld0 = [P.ev("gpsimd"), P.ev("sync"), evA]
        P.mbar()
        if phase_limit <= 1:
            P.barrier(); P.emit(block)
            return nc, dbg_outs, P

        sb_qm = arena.alloc("qm", 4 * 2048, BF16)
        sb_km = arena.alloc("km", 4 * 2048, BF16)
        sb_vp = arena.alloc("vp", 64 * 512, BF16)
        sb_xld = arena.sub("stream", 0, 2 * 1024, BF16)
        sb_lntm = arena.sub("stream", 2048, 2 * 1024, F32)
        sb_xlT = arena.sub("stream", 6144, 8 * 512, BF16)
        sb_vsc = arena.sub("stream", 10240, 512, F32)
        sb_stat = arena.alloc("stat", 16, F32)

        xld_user = {0: None, 1: None}
        lntm_user = {0: None, 1: None}

        def window(src_h, w_src, do_v, vslot, do_qk, deps_in):
            if p1_mode >= 1:
                do_v = False
                do_qk = False
            tp_evs = []
            for j in range(4):
                rb = j % 2
                xt = sb_xld[:, rb * 1024:(rb + 1) * 1024]
                lt = sb_lntm[:, rb * 1024:(rb + 1) * 1024]
                dep = list(deps_in)
                if xld_user[rb] is not None:
                    dep.append(xld_user[rb])
                evl = P.dma("gpsimd", xt,
                            src_h[(w_src * 4 + j) * 128:(w_src * 4 + j + 1) * 128, :],
                            after=dep)
                dep2 = [evl]
                if lntm_user[rb] is not None:
                    dep2.append(lntm_user[rb])
                evn = ln_tile(xt, lt, sb_stat, after=dep2)
                xld_user[rb] = evn
                if p1_mode >= 2:
                    tp_evs.append(evn)
                else:
                    evs = tp_tile_to_fmajor(lt, sb_xlT, lambda fc: fc * 512, j, [evn])
                    lntm_user[rb] = evs[-1]
                    tp_evs += evs
                P.mbar()
            deps = tp_evs
            if do_v:
                for clo in range(8):
                    mmevs = []
                    for ifc in range(8):
                        mmevs.append(mm(psB[:, 0:512],
                                        sb_Mv[:, ifc * 1024 + clo * 128:
                                              ifc * 1024 + clo * 128 + 128],
                                        sb_xlT[:, ifc * 512:(ifc + 1) * 512],
                                        start=(ifc == 0), stop=(ifc == 7),
                                        after=deps if ifc == 0 else []))
                    eve = scopy(sb_vsc, psB[:, 0:512], after=[mmevs[-1]])
                    tevs = []
                    for ttc in range(4):
                        tevs.append(tpose(psB[:, 512 + ttc * 128:512 + ttc * 128 + 128],
                                          sb_vsc[:, ttc * 128:(ttc + 1) * 128],
                                          after=[eve]))
                    evv = scopy(sb_vp[:, (vslot * 8 + clo) * 512:
                                      (vslot * 8 + clo) * 512 + 512],
                                psB[:, 512:1024], after=tevs)
                    deps = [evv]
                    P.mbar()
            if do_qk:
                for (msb, dst) in ((sb_Mq, sb_qm), (sb_Mk, sb_km)):
                    for oc in range(4):
                        mmevs = []
                        for ifc in range(8):
                            mmevs.append(mm(
                                psB[:, 1024:1536],
                                msb[:, ifc * 512 + oc * 128:ifc * 512 + oc * 128 + 128],
                                sb_xlT[:, ifc * 512:(ifc + 1) * 512],
                                start=(ifc == 0), stop=(ifc == 7),
                                after=deps if ifc == 0 else []))
                        ev = scopy(dst[:, oc * 2048 + w_src * 512:
                                       oc * 2048 + w_src * 512 + 512],
                                   psB[:, 1024:1536], after=[mmevs[-1]])
                        deps = [ev]
                        P.mbar()
            return deps

        prev = ld0
        for w in range(min(8, p1_windows)):
            prev = window(x_full_h, w, True, w, False, prev)
        for w in range(max(0, min(4, p1_windows - 8))):
            prev = window(x_own_h, w, False, None, True, prev)

        P.barrier()
        for nm in ("Mq", "Mk", "Mv"):
            arena.free(nm)

        if "qm" in dbg:
            o = dout("qm", (512, T), BF16)
            for oc in range(4):
                P.dma("sync", o[oc * 128:(oc + 1) * 128, :],
                      sb_qm[:, oc * 2048:(oc + 1) * 2048])
        if "km" in dbg:
            o = dout("km", (512, T), BF16)
            for oc in range(4):
                P.dma("sync", o[oc * 128:(oc + 1) * 128, :],
                      sb_km[:, oc * 2048:(oc + 1) * 2048])
        if "vp" in dbg:
            P.dma("sync", dout("vp", (128, 64 * 512), BF16), sb_vp)
        if phase_limit <= 2:
            P.barrier(); P.emit(block)
            return nc, dbg_outs, P

        sb_Kp = arena.alloc("Kp", 4096, BF16)
        sb_Qp = arena.alloc("Qp", 4096, BF16)
        sb_E = arena.alloc("E", 2048, BF16)
        sb_colsum = arena.alloc("colsum", 64, F32)
        sb_lnbias = arena.alloc("lnbias", 32, F32)
        sb_ao = arena.alloc("ao", 4 * 512, F32)
        sb_xres = arena.alloc("xres", 2 * 512, F32)

        res_user = {0: None, 1: None}
        head_end = []
        for g in range(4):
            evKQ = []
            for cb in range(8):
                oc, pofs = cb // 2, (cb % 2) * 64
                evKQ.append(P.dma(
                    "sync", sb_Kp[0:64, cb * 512:(cb + 1) * 512],
                    sb_km[pofs:pofs + 64, oc * 2048 + g * 512:oc * 2048 + g * 512 + 512],
                    after=head_end))
                evKQ.append(P.dma(
                    "sync", sb_Qp[0:64, cb * 512:(cb + 1) * 512],
                    sb_qm[pofs:pofs + 64, oc * 2048 + g * 512:oc * 2048 + g * 512 + 512],
                    after=head_end))

            prev1 = evKQ
            for c2 in range(32):
                lhsT = sb_Kp[0:64, c2 * 128:(c2 + 1) * 128]
                for hh in range(2):
                    mmevs = []
                    for cb in range(4):
                        rhs = sb_Qp[0:64, (hh * 4 + cb) * 512:(hh * 4 + cb + 1) * 512]
                        mmevs.append(mm(psA[:, cb * 512:(cb + 1) * 512], lhsT, rhs,
                                        start=True, stop=True,
                                        after=prev1 if cb == 0 else []))
                    eva = act("scalar", sb_E, psA[:, 0:2048], AF.Exp, scale=0.125,
                              accum=sb_colsum[:, c2 * 2 + hh:c2 * 2 + hh + 1],
                              after=[mmevs[-1]])
                    prev1 = [eva]
                    P.mbar()
            cs = sb_colsum.rearrange("p (c two) -> p c two", two=2)
            ev1 = vadd(sb_lnbias[:, 0:32], cs[:, :, 0], cs[:, :, 1], after=prev1)
            ev2 = act("scalar", sb_lnbias[:, 0:32], sb_lnbias[:, 0:32], AF.Ln,
                      after=[ev1])
            ev3 = vts(sb_lnbias[:, 0:32], sb_lnbias[:, 0:32], -1.0, None,
                      ALU.mult, after=[ev2])

            prev2 = [ev3]
            head_end = []
            for grp in range(2):
                for c2 in range(32):
                    clo, ttc = c2 // 4, c2 % 4
                    lhsT = sb_Kp[0:64, c2 * 128:(c2 + 1) * 128]
                    mmevs = []
                    for cb in range(4):
                        h2 = grp * 4 + cb
                        rhs = sb_Qp[0:64, h2 * 512:(h2 + 1) * 512]
                        mmevs.append(mm(psA[:, cb * 512:(cb + 1) * 512], lhsT, rhs,
                                        start=True, stop=True,
                                        after=prev2 if cb == 0 else []))
                    eva = act("scalar", sb_E, psA[:, 0:2048], AF.Exp, scale=0.125,
                              bias=sb_lnbias[:, c2:c2 + 1], after=[mmevs[-1]])
                    oevs = []
                    for cb in range(4):
                        h2 = grp * 4 + cb
                        lhsT2 = sb_vp[:, (h2 * 8 + clo) * 512 + ttc * 128:
                                      (h2 * 8 + clo) * 512 + ttc * 128 + 128]
                        oevs.append(mm(psB[:, cb * 512:(cb + 1) * 512], lhsT2,
                                       sb_E[:, cb * 512:(cb + 1) * 512],
                                       start=(c2 == 0), stop=(c2 == 31),
                                       after=[eva]))
                    prev2 = [oevs[-1]]
                    P.mbar()
                evacs = []
                for cb in range(4):
                    evacs.append(scopy(sb_ao[:, cb * 512:(cb + 1) * 512],
                                       psB[:, cb * 512:(cb + 1) * 512], after=prev2))
                frange = grp * 512
                res_evs = []
                for ts_ in range(4):
                    tile = g * 4 + ts_
                    rb = ts_ % 2
                    dep = [res_user[rb]] if res_user[rb] else []
                    evl = P.dma("sync", sb_xres[:, rb * 512:(rb + 1) * 512],
                                x_own_h[tile * 128:(tile + 1) * 128,
                                        frange:frange + 512], after=dep)
                    tevs = []
                    for cb in range(4):
                        tevs.append(tpose(psA[:, ts_ * 512 + cb * 128:
                                              ts_ * 512 + cb * 128 + 128],
                                          sb_ao[:, cb * 512 + ts_ * 128:
                                                cb * 512 + ts_ * 128 + 128],
                                          after=[evacs[cb]] + prev2))
                    evr = vadd(sb_stream[:, tile * 1024 + frange:
                                         tile * 1024 + frange + 512],
                               psA[:, ts_ * 512:ts_ * 512 + 512],
                               sb_xres[:, rb * 512:(rb + 1) * 512],
                               after=tevs + [evl])
                    res_user[rb] = evr
                    res_evs.append(evr)
                    P.mbar()
                prev2 = res_evs
                head_end = res_evs

        P.barrier()
        for nm in ("Kp", "Qp", "E", "colsum", "lnbias", "ao", "xres", "qm", "km", "vp"):
            arena.free(nm)

        if "x1" in dbg:
            o = dout("x1", (T, D))
            for t in range(NT):
                P.dma("sync", o[t * 128:(t + 1) * 128, :],
                      sb_stream[:, t * 1024:(t + 1) * 1024])
        if phase_limit <= 3:
            P.barrier(); P.emit(block)
            return nc, dbg_outs, P

        sb_G1X = arena.alloc("G1X", 4 * 32, F32)
        sb_G2X = arena.alloc("G2X", 16, F32)
        sb_Wacc = arena.alloc("Wacc", 4 * 512, F32)
        sb_Wtmp = arena.alloc("Wtmp", 512, F32)
        sb_WTr = arena.alloc("WTr", 4 * 512, BF16)
        sb_WTi = arena.alloc("WTi", 4 * 512, BF16)
        sb_WTin = arena.alloc("WTin", 4 * 512, BF16)

        def build_WT(g1h, g2h, dstr, dst_neg, start_dep=()):
            accev = list(start_dep)
            for cc in range(4):
                g1rep = g1h[:, cc, :].rearrange("b d -> d b")
                evs = []
                for ch in range(4):
                    for dd in range(8):
                        srcap = g1rep[8 * ch + dd:8 * ch + dd + 1, :] \
                            .broadcast_to([16, 32])
                        evs.append(P.dma(
                            "sync", sb_G1X[dd * 16:(dd + 1) * 16,
                                           ch * 32:(ch + 1) * 32],
                            srcap, after=accev, slow=True))
                g2src = g2h[cc, :, :].rearrange("e g -> g e")
                for r in range(8):
                    evs.append(P.dma("sync", sb_G2X[r * 16:(r + 1) * 16, 0:16],
                                     g2src, after=accev, slow=True))
                accev = []
                for ch in range(4):
                    in0 = sb_G1X[:, ch * 32:(ch + 1) * 32] \
                        .rearrange("p (b o) -> p b o", o=1).broadcast_to([128, 32, 16])
                    in1 = sb_G2X.rearrange("p (o e) -> p o e", o=1) \
                        .broadcast_to([128, 32, 16])
                    if cc == 0:
                        ev = P.op("vector", lambda e, ch=ch, in0=in0, in1=in1:
                                  e.tensor_tensor(
                                      sb_Wacc[:, ch * 512:(ch + 1) * 512]
                                      .rearrange("p (b e) -> p b e", b=32),
                                      in0, in1, ALU.mult), after=evs + accev)
                    else:
                        ev = P.op("vector", lambda e, in0=in0, in1=in1:
                                  e.tensor_tensor(
                                      sb_Wtmp.rearrange("p (b e) -> p b e", b=32),
                                      in0, in1, ALU.mult), after=evs + accev)
                        ev = vadd(sb_Wacc[:, ch * 512:(ch + 1) * 512],
                                  sb_Wacc[:, ch * 512:(ch + 1) * 512], sb_Wtmp,
                                  after=[ev])
                    accev.append(ev)
            ev = vcopy(dstr, sb_Wacc, after=accev)
            if dst_neg is not None:
                ev = vts(dst_neg, sb_Wacc, -1.0, None, ALU.mult, after=[ev])
            return ev

        evWr = build_WT(G1R_h, G2R_h, sb_WTr, None)
        evWi = build_WT(G1I_h, G2I_h, sb_WTi, sb_WTin, start_dep=[evWr])

        sb_bRI = arena.alloc("bRI", 8, F32)
        for c in range(4):
            P.dma("sync", sb_bRI[:, c:c + 1],
                  bR_h[c * 128:(c + 1) * 128].rearrange("(p o) -> p o", o=1))
            P.dma("sync", sb_bRI[:, 4 + c:5 + c],
                  bI_h[c * 128:(c + 1) * 128].rearrange("(p o) -> p o", o=1))

        sb_ln2 = arena.alloc("ln2", 2 * 1024, F32)
        sb_x2T = arena.alloc("x2T", 8 * 512, BF16)
        sb_ffs = arena.alloc("ffs", 512, F32)

        prevW = [evWr, evWi, P.ev("sync")]
        ln2_user = {0: None, 1: None}
        for chk in range(4):
            tp_evs = []
            for j in range(4):
                tile = chk * 4 + j
                rb = j % 2
                lt = sb_ln2[:, rb * 1024:(rb + 1) * 1024]
                dep = ([ln2_user[rb]] if ln2_user[rb] else []) + prevW
                evn = ln_tile(sb_stream[:, tile * 1024:(tile + 1) * 1024], lt,
                              sb_stat, after=dep)
                evs = tp_tile_to_fmajor(lt, sb_x2T, lambda fc: fc * 512, j, [evn])
                ln2_user[rb] = evs[-1]
                tp_evs += evs
                P.mbar()
            prevW = []
            psB_free = tp_evs
            for oc in range(8):
                is_im = oc >= 4
                ocl = oc % 4
                mmevs = []
                for k in range(8):
                    dgc = k % 4
                    from_im = k >= 4
                    wsb = (sb_WTin if (not is_im and from_im) else
                           sb_WTr if (not is_im or from_im) else sb_WTi)
                    lhsT = wsb[:, dgc * 512 + ocl * 128: dgc * 512 + ocl * 128 + 128]
                    rhs = sb_x2T[:, (4 * 512 if from_im else 0) + dgc * 512:
                                 (4 * 512 if from_im else 0) + dgc * 512 + 512]
                    mmevs.append(mm(psB[:, 0:512], lhsT, rhs,
                                    start=(k == 0), stop=(k == 7),
                                    after=psB_free if k == 0 else []))
                eve = act("scalar", sb_ffs, psB[:, 0:512], AF.Identity,
                          bias=sb_bRI[:, oc:oc + 1], after=[mmevs[-1]])
                psB_free = [eve]
                radd = []
                for ts_ in range(4):
                    tile = chk * 4 + ts_
                    sl = ts_ % 2
                    pslice = psA[:, sl * 128:(sl + 1) * 128]
                    dep = [eve] + (radd[-2:-1] if len(radd) >= 2 else [])
                    et = tpose(pslice, sb_ffs[:, ts_ * 128:(ts_ + 1) * 128],
                               after=dep)
                    fcol = oc * 128
                    evr = vadd(sb_stream[:, tile * 1024 + fcol:
                                         tile * 1024 + fcol + 128],
                               pslice,
                               sb_stream[:, tile * 1024 + fcol:
                                         tile * 1024 + fcol + 128],
                               after=[et])
                    radd.append(evr)
                psB_free = psB_free + radd
                prevW = radd
                P.mbar()

        P.barrier()
        for nm in ("G1X", "G2X", "Wacc", "Wtmp", "WTr", "WTi", "WTin", "bRI",
                   "ln2", "x2T", "ffs"):
            arena.free(nm)

        if "x2" in dbg:
            o = dout("x2", (T, D))
            for t in range(NT):
                P.dma("sync", o[t * 128:(t + 1) * 128, :],
                      sb_stream[:, t * 1024:(t + 1) * 1024])
        if phase_limit <= 4:
            P.barrier(); P.emit(block)
            return nc, dbg_outs, P

        ev = P.dma("sync", cc_halo_in.ap(), sb_stream[125:128, 15 * 1024:16 * 1024])
        evcc = P.op("gpsimd", lambda e: e.collective_compute(
            "AllGather", ALU.bypass, ins=[cc_halo_in.ap()],
            outs=[cc_halo_out.ap()], replica_groups=[[0, 1, 2, 3, 4, 5, 6, 7]]),
            after=[ev])

        sb_sel16 = arena.alloc("sel16", 16 * 128, BF16)
        evsel = P.dma("gpsimd", sb_sel16[0:16, :], c_sel16.ap())
        sb_Woutr = arena.alloc("Woutr", 3 * 1024, BF16)
        sb_Winr = arena.alloc("Winr", 3 * 1024, BF16)
        sb_x3T = arena.alloc("x3T", 8 * 512, BF16)
        sb_xcn = arena.alloc("xcn", 16 * 512, BF16)
        sb_zs = arena.alloc("zs", 16 * 512, BF16)
        sb_acc = arena.alloc("acc", 16 * 512, F32)
        sb_y = arena.alloc("y", 16 * 512, BF16)
        sb_ln3 = arena.sub("y", 0, 2 * 1024, F32)
        sb_carry = arena.alloc("carry", 16 * 4, BF16)
        sb_xcp = arena.alloc("xcp", 516, BF16)
        sb_csc = arena.alloc("csc", 512, BF16)
        sb_sq = arena.alloc("sq", 512, BF16)
        sb_xph = arena.alloc("xph", 512, F32)
        sb_mrow = arena.alloc("mrow", 512, F32)
        sb_vrow = arena.alloc("vrow", 512, F32)
        sb_rrow = arena.alloc("rrow", 512, F32)
        sb_rowb = arena.alloc("rowb", 1024, BF16)
        sb_dbc = arena.alloc("dbc", 512, BF16)
        sb_bcf = arena.alloc("bcf", 2 * 512, F32)
        sb_bcb = arena.alloc("bcb", 512, BF16)
        sb_term = arena.alloc("term", 512, F32)
        sb_h24 = arena.sub("acc", 0, 1024, F32)
        sb_h24b = arena.sub("acc", 2048, 1024, BF16)
        sb_h24c = arena.sub("acc", 3072, 1024, F32)
        sb_x3hT = arena.alloc("x3hT", 8 * 4, BF16)

        evh = P.dma("sync", sb_h24[0:24, :], cc_halo_out.ap(), after=[evcc])
        evhb = vcopy(sb_h24b[0:24, :], sb_h24[0:24, :], after=[evh])
        hsel = []
        for mo in range(2):
            hsel.append(mm(psA[0:3, mo * 512:(mo + 1) * 512],
                           sb_halosel[0:24, 0:3],
                           sb_h24b[0:24, mo * 512:(mo + 1) * 512],
                           start=True, stop=True, after=[evhb]))
        evx2h = scopy(sb_h24[0:3, :], psA[0:3, 0:1024], after=hsel)
        evlnh = ln_tile(sb_h24[0:3, :], sb_h24c[0:3, :], sb_stat, after=[evx2h])
        hts = []
        for fc in range(8):
            et = tpose(psB[:, 1536:1539], sb_h24c[0:3, fc * 128:(fc + 1) * 128],
                       after=[evlnh] + hts[-1:])
            ee = scopy(sb_x3hT[:, fc * 4:fc * 4 + 3], psB[:, 1536:1539], after=[et])
            hts.append(ee)

        carry_evs = {}
        winr_user = [None, None, None]
        ln3_user = {0: None, 1: None}
        prev5 = [evA] + hts
        prev5w = []
        for chk in range(4):
            tp_evs = []
            for j in range(4):
                tile = chk * 4 + j
                rb = j % 2
                lt = sb_ln3[:, rb * 1024:(rb + 1) * 1024]
                dep = ([ln3_user[rb]] if ln3_user[rb] else []) + prev5
                evn = ln_tile(sb_stream[:, tile * 1024:(tile + 1) * 1024], lt,
                              sb_stat, after=dep)
                evs = tp_tile_to_fmajor(lt, sb_x3T, lambda fc: fc * 512, j, [evn])
                ln3_user[rb] = evs[-1]
                tp_evs += evs
                P.mbar()
            prev5 = []

            psB_user = tp_evs
            psH_user = hts
            evmu = []
            zsil = []
            for zc in range(32):
                rslot = zc % 3
                dep = [winr_user[rslot]] if winr_user[rslot] else []
                evw = P.dma("gpsimd",
                            sb_Winr[:, rslot * 1024:(rslot + 1) * 1024]
                            .rearrange("p (c f) -> p c f", f=128),
                            Win_h[:, zc * 128:(zc + 1) * 128]
                            .rearrange("(c p) f -> p c f", p=128),
                            after=dep)
                mmevs = []
                for ifc in range(8):
                    aft = []
                    if ifc == 0:
                        aft = tp_evs + [evw] + psB_user
                    mmevs.append(mm(psB[:, 0:512],
                                    sb_Winr[:, rslot * 1024 + ifc * 128:
                                            rslot * 1024 + ifc * 128 + 128],
                                    sb_x3T[:, ifc * 512:(ifc + 1) * 512],
                                    start=(ifc == 0), stop=(ifc == 7), after=aft))
                winr_user[rslot] = mmevs[-1]
                if zc < 16:
                    if chk == 0:
                        hmm = []
                        for ifc in range(8):
                            hmm.append(mm(psB[:, 512:515],
                                          sb_Winr[:, rslot * 1024 + ifc * 128:
                                                  rslot * 1024 + ifc * 128 + 128],
                                          sb_x3hT[:, ifc * 4:ifc * 4 + 3],
                                          start=(ifc == 0), stop=(ifc == 7),
                                          after=psH_user if ifc == 0 else []))
                        winr_user[rslot] = hmm[-1]
                        evcar = scopy(sb_carry[:, zc * 4:zc * 4 + 3],
                                      psB[:, 512:515], after=[hmm[-1]])
                        carry_evs[zc] = evcar
                        psH_user = [evcar]
                    evc0 = vcopy(sb_xcp[:, 0:3], sb_carry[:, zc * 4:zc * 4 + 3],
                                 after=[carry_evs[zc]])
                    evc1 = scopy(sb_xcp[:, 3:515], psB[:, 0:512],
                                 after=[mmevs[-1], carry_evs[zc]])
                    psB_user = [evc1]
                    evcar = vcopy(sb_carry[:, zc * 4:zc * 4 + 3],
                                  sb_xcp[:, 512:515], after=[evc1, evc0])
                    carry_evs[zc] = evcar
                    evcv = vts(sb_csc, sb_xcp[:, 0:512],
                               sb_convw[:, zc * 4:zc * 4 + 1], None, ALU.mult,
                               after=[evc0, evc1])
                    for jt in range(1, 4):
                        evcv = vstt(sb_csc, sb_xcp[:, jt:jt + 512],
                                    sb_convw[:, zc * 4 + jt:zc * 4 + jt + 1],
                                    sb_csc, ALU.mult, ALU.add, after=[evcv])
                    evsl = act("scalar", sb_xcn[:, zc * 512:(zc + 1) * 512], sb_csc,
                               AF.Silu, after=[evcv])
                    evsq = act("scalar", sb_sq, sb_xcn[:, zc * 512:(zc + 1) * 512],
                               AF.Square, after=[evsl])
                    evmu.append(mm(psA[0:1, 1024:1536], sb_lnsc,
                                   sb_xcn[:, zc * 512:(zc + 1) * 512],
                                   start=(zc == 0), stop=(zc == 15), after=[evsl]))
                    evmu.append(mm(psA[0:1, 512:1024], sb_lnsc, sb_sq,
                                   start=(zc == 0), stop=(zc == 15), after=[evsq]))
                else:
                    ev = act("scalar", sb_zs[:, (zc - 16) * 512:(zc - 15) * 512],
                             psB[:, 0:512], AF.Silu, after=[mmevs[-1]])
                    psB_user = [ev]
                    zsil.append(ev)
                P.mbar()
            evm = scopy(sb_mrow[0:1, :], psA[0:1, 1024:1536], after=evmu)
            evv0 = scopy(sb_vrow[0:1, :], psA[0:1, 512:1024], after=evmu)
            evm2 = vtt(sb_rrow[0:1, :], sb_mrow[0:1, :], sb_mrow[0:1, :],
                       ALU.mult, after=[evm])
            evvar = vtt(sb_vrow[0:1, :], sb_vrow[0:1, :], sb_rrow[0:1, :],
                        ALU.subtract, after=[evv0, evm2])
            evr1 = act("scalar", sb_rrow[0:1, :], sb_vrow[0:1, :], AF.Ln,
                       bias=sb_eps[0:1, 0:1], after=[evvar])
            evr2 = act("scalar", sb_rrow[0:1, :], sb_rrow[0:1, :], AF.Exp,
                       scale=-0.5, after=[evr1])
            evbm = vcopy(sb_rowb[0:1, 0:512], sb_mrow[0:1, :], after=[evm2])
            evbr = vcopy(sb_rowb[0:1, 512:1024], sb_rrow[0:1, :], after=[evr2])
            evb1 = mm(psA[:, 1024:1536], sb_ones1b[0:1, :], sb_rowb[0:1, 0:512],
                      start=True, stop=True, after=[evbm] + evmu)
            evb2 = mm(psA[:, 1536:2048], sb_ones1b[0:1, :], sb_rowb[0:1, 512:1024],
                      start=True, stop=True, after=[evbr] + [tp_user[0], tp_user[1]])
            evn_l = []
            for zc in range(16):
                blk = sb_xcn[:, zc * 512:(zc + 1) * 512]
                e1 = vstt(sb_csc, psA[:, 1024:1536], -1.0, blk, ALU.mult, ALU.add,
                          after=[evb1, evb2] + evn_l[-1:])
                e2 = vtt(sb_csc, sb_csc, psA[:, 1536:2048], ALU.mult, after=[e1])
                e3 = act("scalar", blk, sb_csc, AF.Identity,
                         bias=sb_lnbeta[:, zc:zc + 1],
                         scale=sb_lng[:, zc:zc + 1], after=[e2])
                evn_l.append(e3)
                P.mbar()
            xpe = []
            for zc in range(16):
                xpe.append(mm(psB[0:33, 0:512], sb_Wxp[:, zc * 33:(zc + 1) * 33],
                              sb_xcn[:, zc * 512:(zc + 1) * 512],
                              start=(zc == 0), stop=(zc == 15),
                              after=(evn_l + psB_user) if zc == 0 else [evn_l[zc]]))
            evxp = scopy(sb_xph[0:33, :], psB[0:33, 0:512], after=[xpe[-1]])
            psB_user = [evxp]
            evd = softplus(sb_mrow[0:1, :], sb_xph[0:1, :], sb_vrow[0:1, :],
                           after=[evxp, evb1])
            evd = vts(sb_mrow[0:1, :], sb_mrow[0:1, :], 1e-6, 10.0,
                      ALU.max, ALU.min, after=[evd])
            evdb = vcopy(sb_rowb[0:1, 0:512], sb_mrow[0:1, :], after=[evd, evb1])
            evdb2 = mm(psA[:, 1024:1536], sb_ones1b[0:1, :], sb_rowb[0:1, 0:512],
                       start=True, stop=True, after=[evdb] + evn_l)
            evdbc = scopy(sb_dbc, psA[:, 1024:1536], after=[evdb2])
            evs1 = P.dma("sync", sb_bcf[0:16, 0:512], sb_xph[1:17, :],
                         after=[evxp])
            evs2 = P.dma("sync", sb_bcf[0:16, 512:1024], sb_xph[17:33, :],
                         after=[evxp])
            evbcb = vtt(sb_bcb[0:16, :], sb_bcf[0:16, 0:512],
                        sb_bcf[0:16, 512:1024], ALU.mult, after=[evs1, evs2])
            accw = {}
            bc_user = []
            for n in range(16):
                evn0 = mm(psB[:, 1536:2048], sb_sel16[0:16, n * 128:(n + 1) * 128],
                          sb_bcb[0:16, :], start=True, stop=True,
                          after=[evbcb, evdbc] + bc_user)
                bc_user = []
                for ic in range(16):
                    evx = act("scalar", sb_csc, sb_dbc, AF.Exp,
                              scale=sb_A[:, ic * 16 + n:ic * 16 + n + 1],
                              after=[evn0] + (bc_user[-1:] if bc_user else evn_l))
                    evt = vstt(sb_term, sb_csc, 1e-6, psB[:, 1536:2048],
                               ALU.max, ALU.mult, after=[evx])
                    if n == 0:
                        accw[ic] = vcopy(sb_acc[:, ic * 512:(ic + 1) * 512],
                                         sb_term, after=[evt])
                    else:
                        accw[ic] = vadd(sb_acc[:, ic * 512:(ic + 1) * 512],
                                        sb_acc[:, ic * 512:(ic + 1) * 512],
                                        sb_term, after=[evt, accw[ic]])
                    bc_user.append(accw[ic])
                P.mbar()
            if chk == 0 and "p5dbg" in dbg:
                ev = P.dma("sync", dout("xcn0", (128, 16 * 512), BF16), sb_xcn,
                           after=evn_l + [evxp])
                prev5.append(ev)
                ev = P.dma("sync", dout("acc0", (128, 16 * 512)), sb_acc,
                           after=[accw[ic] for ic in range(16)])
                prev5.append(ev)
                ev = P.dma("sync", dout("xph0", (33, 512)), sb_xph[0:33, :],
                           after=[evxp, evd, evbcb])
                prev5.append(ev)
                ev = P.dma("sync", dout("zs0", (128, 16 * 512), BF16), sb_zs,
                           after=zsil)
                prev5.append(ev)
            ye = []
            for ic in range(16):
                e1 = vstt(sb_y[:, ic * 512:(ic + 1) * 512],
                          sb_acc[:, ic * 512:(ic + 1) * 512],
                          sb_Dv[:, ic:ic + 1],
                          sb_xcn[:, ic * 512:(ic + 1) * 512],
                          ALU.add, ALU.mult, after=[accw[ic]])
                e2 = vtt(sb_y[:, ic * 512:(ic + 1) * 512],
                         sb_y[:, ic * 512:(ic + 1) * 512],
                         sb_zs[:, ic * 512:(ic + 1) * 512], ALU.mult,
                         after=[e1] + zsil)
                ye.append(e2)
            pso = [psA[:, 0:1024], psA[:, 1024:2048],
                   psB[:, 0:1024], psB[:, 1024:2048]]
            wor_user = [None, None, None]
            mmlast = [None] * 4
            for ic in range(16):
                rs = ic % 3
                dep = [wor_user[rs]] if wor_user[rs] else []
                evw = P.dma("gpsimd", sb_Woutr[:, rs * 1024:(rs + 1) * 1024],
                            Wout_h[ic * 128:(ic + 1) * 128, :], after=dep)
                for ts_ in range(4):
                    for mo in range(2):
                        aft = [evw]
                        if ic == 0 and ts_ == 0 and mo == 0:
                            aft = aft + ye + psB_user + [evdbc] + prev5w
                        mmlast[ts_] = mm(pso[ts_][:, mo * 512:(mo + 1) * 512],
                                         sb_y[:, ic * 512 + ts_ * 128:
                                              ic * 512 + ts_ * 128 + 128],
                                         sb_Woutr[:, rs * 1024 + mo * 512:
                                                  rs * 1024 + mo * 512 + 512],
                                         start=(ic == 0), stop=(ic == 15),
                                         after=aft if mo == 0 and ts_ == 0 else [])
                wor_user[rs] = mmlast[3]
            prev5 = []
            for ts_ in range(4):
                tile = chk * 4 + ts_
                ev1 = vadd(sb_stream[:, tile * 1024:(tile + 1) * 1024],
                           pso[ts_], sb_stream[:, tile * 1024:(tile + 1) * 1024],
                           after=[mmlast[ts_]])
                prev5.append(ev1)
            prev5w = prev5
            P.mbar()

        P.barrier()
        for nm in ("Winr", "Woutr", "x3T", "xcn", "zs", "acc", "y",
                   "carry", "xcp", "csc", "sq", "xph", "mrow", "vrow", "rrow",
                   "rowb", "dbc", "bcf", "bcb", "term", "x3hT",
                   "A", "convw", "lng", "lnbeta", "Dv", "Wxp", "halosel"):
            arena.free(nm)

        if "x3" in dbg:
            o = dout("x3", (T, D))
            for t in range(NT):
                P.dma("sync", o[t * 128:(t + 1) * 128, :],
                      sb_stream[:, t * 1024:(t + 1) * 1024])
        if phase_limit <= 5:
            P.barrier(); P.emit(block)
            return nc, dbg_outs, P

        sb_xn4 = arena.alloc("xn4", 16 * 1024, BF16)
        sb_lg = arena.alloc("lg", 2 * 1024, F32)
        sb_lgT = arena.alloc("lgT", 1024, F32)
        sb_logit = arena.alloc("logit", 16 * 8, F32)
        sb_esc = arena.alloc("esc", 8, F32)
        sb_g1 = arena.alloc("g1", 16, F32)
        sb_mt = arena.alloc("mt", 8, F32)
        sb_stat2 = arena.alloc("stat2", 8, F32)
        sb_mem = arena.alloc("mem", 2048, F32)
        sb_incl = arena.alloc("incl", 2048, F32)
        sb_lfr = arena.alloc("lfr", 2048, F32)
        sb_rowpair = arena.alloc("rowpair", 2 * 2048, F32)
        sb_lfrow = sb_rowpair[:, 0:2048]
        kept_row = sb_rowpair[0:1, 2048:4096]
        sb_lfc = arena.alloc("lfc", 16, F32)
        sb_kg = arena.alloc("kg", 16, F32)

        ln4_user = {0: None, 1: None}
        mem_evs = []
        lg_user = {0: None, 1: None}
        mm8_user = []
        mt_user = []
        for t in range(NT):
            rb = t % 2
            xt = sb_stream[:, t * 1024:(t + 1) * 1024]
            dep = [ln4_user[rb]] if ln4_user[rb] else []
            evn = ln_tile(xt, sb_xn4[:, t * 1024:(t + 1) * 1024], sb_stat,
                          after=dep)
            ln4_user[rb] = evn
            lgt = sb_lg[:, rb * 1024:(rb + 1) * 1024]
            dep = [lg_user[rb]] if lg_user[rb] else []
            evf = ln_tile(xt, lgt, sb_stat, after=[evn] + dep)
            mme = []
            lastev = [evf]
            for fc in range(8):
                sl = fc % 2
                pslice = psA[:, 1024 + sl * 128:1024 + sl * 128 + 128]
                et = tpose(pslice, lgt[:, fc * 128:(fc + 1) * 128],
                           after=lastev, f32=True)
                ee = scopy(sb_lgT[:, fc * 128:(fc + 1) * 128], pslice, after=[et])
                lastev = [ee]
                mme.append(mm(psB[:, 0:8], sb_lgT[:, fc * 128:(fc + 1) * 128],
                              sb_wg[:, fc * 8:(fc + 1) * 8],
                              start=(fc == 0), stop=(fc == 7),
                              after=([ee] + mm8_user) if fc == 0 else [ee]))
            lg_user[rb] = lastev[0]
            evlg = scopy(sb_logit[:, t * 8:(t + 1) * 8], psB[:, 0:8],
                         after=[mme[-1]])
            mm8_user = [evlg]
            lcol = sb_logit[:, t * 8:(t + 1) * 8]
            evmx = P.op("vector", lambda e, lcol=lcol: e.reduce_max(
                sb_stat2[:, 0:1], lcol, axis=AX.X), after=[evlg] + mt_user)
            evmk = vts(sb_mt, lcol, sb_stat2[:, 0:1], None, ALU.is_equal,
                       after=[evmx])
            evnm = vts(sb_stat2[:, 1:2], sb_stat2[:, 0:1], -1.0, None, ALU.mult,
                       after=[evmx])
            evex = act("scalar", sb_esc, lcol, AF.Exp, bias=sb_stat2[:, 1:2],
                       accum=sb_stat2[:, 2:3], after=[evnm])
            evrc = P.op("vector", lambda e, t=t: e.reciprocal(
                sb_g1[:, t:t + 1], sb_stat2[:, 2:3]), after=[evex])
            et = tpose(psA[0:8, 1280:1408], sb_mt, after=[evmk], f32=True)
            ee = scopy(sb_mem[0:8, t * 128:(t + 1) * 128], psA[0:8, 1280:1408],
                       after=[et])
            mt_user = [ee, evrc]
            mem_evs.append(ee)
            mem_evs.append(evrc)
            P.mbar()

        if "p6dbg" in dbg:
            P.dma("sync", dout("lgt", (128, 128)), sb_logit, after=mem_evs)
            P.dma("sync", dout("g1d", (128, 16)), sb_g1, after=mem_evs)
            P.dma("sync", dout("memd", (8, 2048)), sb_mem[0:8, :], after=mem_evs)
        evscan = P.op("vector", lambda e: e.tensor_tensor_scan(
            sb_incl[0:8, :], sb_mem[0:8, :], sb_mem[0:8, :], 0.0,
            op0=ALU.add, op1=ALU.bypass), after=mem_evs)
        evcnt = vcopy(sb_lfr[0:8, 0:1], sb_incl[0:8, 2047:2048], after=[evscan])
        evcd = P.dma("sync", cc_cnt_in.ap(), sb_lfr[0:8, 0:1], after=[evcnt])
        evcc2 = P.op("gpsimd", lambda e: e.collective_compute(
            "AllGather", ALU.bypass, ins=[cc_cnt_in.ap()],
            outs=[cc_cnt_out.ap()], replica_groups=[[0, 1, 2, 3, 4, 5, 6, 7]]),
            after=[evcd])

        evlf1 = vts(sb_lfr[0:8, :], sb_incl[0:8, :], sb_ebase[0:8, :], None,
                    ALU.add, after=[evscan, evcd])
        evlf2 = vtt(sb_lfr[0:8, :], sb_lfr[0:8, :], sb_mem[0:8, :], ALU.mult,
                    after=[evlf1])
        lfm = []
        for mo in range(4):
            lfm.append(mm(psA[0:1, mo * 512:(mo + 1) * 512], sb_ones8[0:8, :],
                          sb_lfr[0:8, mo * 512:(mo + 1) * 512],
                          start=True, stop=True,
                          after=[evlf2] + ([] if mo else mm8_user)))
        evlfr = scopy(sb_lfrow[0:1, :], psA[0:1, 0:2048], after=lfm)
        lfc_evs = []
        for t in range(NT):
            sl = t % 2
            pslice = psA[:, 1024 + sl * 128:1024 + sl * 128 + 1]
            et = tpose(pslice, sb_lfrow[0:1, t * 128:(t + 1) * 128],
                       after=[evlfr] + lfc_evs[-2:-1], f32=True)
            ee = scopy(sb_lfc[:, t:t + 1], pslice, after=[et])
            lfc_evs.append(ee)

        if "p6dbg" in dbg:
            P.dma("sync", dout("incld", (8, 2048)), sb_incl[0:8, :],
                  after=[evscan, evlf2])
            P.dma("sync", dout("lfrowd", (1, 2048)), sb_lfrow[0:1, :],
                  after=[evlfr])
        arena.free("lg"); arena.free("lgT"); arena.free("wg")
        sb_Dm = arena.alloc("Dm", 8 * 512, BF16)
        disp_e = [arena.alloc(f"disp{e}", 8 * 384, BF16) for e in range(6)]
        disp_e.append(arena.sub("lfr", 0, 8 * 384, BF16))
        disp_e.append(arena.sub("rowpair", 4096, 8 * 384, BF16))

        def psd(mc):
            return (psA if mc < 4 else psB)[:, (mc % 4) * 512:(mc % 4) * 512 + 512]

        disp_ev = []
        dstat_user = []
        evac_ev = []
        dm_guard = []
        for sc in range(6):
            mm_half = []
            for half in range(2):
                de = []
                for tt in range(8):
                    t = half * 8 + tt
                    e1 = vts(sb_stat2[:, 3:4], sb_lfc[:, t:t + 1],
                             -(sc * 512.0), None, ALU.add,
                             after=[lfc_evs[t]] + dstat_user)
                    e2 = vts(sb_Dm[:, tt * 512:(tt + 1) * 512], sb_iota512,
                             sb_stat2[:, 3:4], None, ALU.is_equal,
                             after=[e1] + mm_half[-1:] + dm_guard)
                    dstat_user = [e2]
                    de.append(e2)
                for mc in range(8):
                    for tt in range(8):
                        t = half * 8 + tt
                        aft = []
                        if tt == 0:
                            aft = de + (evac_ev + lfc_evs
                                        if (half == 0 and mc == 0) else [])
                        ev = mm(psd(mc),
                                sb_xn4[:, t * 1024 + mc * 128:
                                       t * 1024 + mc * 128 + 128],
                                sb_Dm[:, tt * 512:(tt + 1) * 512],
                                start=(t == 0), stop=(t == 15), after=aft)
                    mm_half.append(ev)
            dm_guard = mm_half[-1:]
            evac_ev = []
            for mc in range(8):
                lo = sc * 512
                while lo < (sc + 1) * 512:
                    e_ = lo // CAPL
                    hi = min((sc + 1) * 512, (e_ + 1) * CAPL)
                    ev = scopy(disp_e[e_][:, mc * 384 + (lo - e_ * CAPL):
                                          mc * 384 + (hi - e_ * CAPL)],
                               psd(mc)[:, (lo - sc * 512):(hi - sc * 512)],
                               after=mm_half[-1:])
                    disp_ev.append(ev)
                    evac_ev.append(ev)
                    lo = hi
            P.mbar()
        P.barrier()
        arena.free("xn4"); arena.free("Dm")

        if "p6dbg" in dbg:
            ev = P.dma("sync", dout("disp0", (128, 8 * 384), BF16), disp_e[0],
                       after=disp_ev)
            disp_ev = disp_ev + [ev]
        sb_h = arena.alloc("h", 32 * 384, BF16)
        sb_w1r = arena.alloc("w1r", 3 * 1024, BF16)
        sb_w2r = arena.alloc("w2r", 3 * 1024, BF16)
        sb_b2bc = arena.alloc("b2bc", 1024, BF16)
        eo_e = [None] * EXPERTS
        w1_user = [None, None, None]
        w2_user = [None, None, None]
        psb384_user = disp_ev
        eo_evs = []
        for e_ in range(EXPERTS):
            hev = []
            for fc in range(32):
                rs = fc % 3
                dep = [w1_user[rs]] if w1_user[rs] else []
                evw = P.dma("gpsimd",
                            sb_w1r[:, rs * 1024:(rs + 1) * 1024]
                            .rearrange("p (c f) -> p c f", f=128),
                            w1_h[e_][:, fc * 128:(fc + 1) * 128]
                            .rearrange("(c p) f -> p c f", p=128),
                            after=dep)
                mmv = []
                for mc in range(8):
                    aft = [evw]
                    if mc == 0:
                        aft = aft + psb384_user
                    mmv.append(mm(psB[:, 0:384],
                                  sb_w1r[:, rs * 1024 + mc * 128:
                                         rs * 1024 + mc * 128 + 128],
                                  disp_e[e_][:, mc * 384:(mc + 1) * 384],
                                  start=(mc == 0), stop=(mc == 7),
                                  after=aft if mc == 0 else []))
                w1_user[rs] = mmv[-1]
                ev = act("scalar", sb_h[:, fc * 384:(fc + 1) * 384], psB[:, 0:384],
                         AF.Gelu_apprx_tanh,
                         bias=sb_b1[:, e_ * 32 + fc:e_ * 32 + fc + 1],
                         after=[mmv[-1]])
                psb384_user = [ev]
                hev.append(ev)
                P.mbar()
            if e_ < 6:
                arena.free(f"disp{e_}")
            if e_ == 7:
                eo_e[e_] = arena.sub("h", 0, 3 * 1024, BF16)
            else:
                eo_e[e_] = arena.alloc(f"eo{e_}", 3 * 1024, BF16)
            evb2m = None
            for mo in range(2):
                evb2m = mm(psA[:, mo * 512:(mo + 1) * 512],
                           sb_sel16[0:8, e_ * 128:(e_ + 1) * 128],
                           sb_b2[0:8, mo * 512:(mo + 1) * 512],
                           start=True, stop=True,
                           after=(eo_evs[-1:] + hev[-1:]) if mo == 0 else [])
            evb2 = scopy(sb_b2bc, psA[:, 0:1024], after=[evb2m])
            pse = [psB[:, 1024:2048], psA[:, 1024:2048], psB[:, 0:1024]]
            mm_last = [None, None, None]
            for fc in range(32):
                rs = fc % 3
                dep = [w2_user[rs]] if w2_user[rs] else []
                evw = P.dma("gpsimd", sb_w2r[:, rs * 1024:(rs + 1) * 1024],
                            w2_h[e_][fc * 128:(fc + 1) * 128, :], after=dep)
                for sc3 in range(3):
                    for mo in range(2):
                        aft = [evw]
                        if fc == 0 and mo == 0:
                            aft = aft + hev + [evb2] + eo_evs[-3:]
                        mm_last[sc3] = mm(pse[sc3][:, mo * 512:(mo + 1) * 512],
                                          sb_h[:, fc * 384 + sc3 * 128:
                                               fc * 384 + sc3 * 128 + 128],
                                          sb_w2r[:, rs * 1024 + mo * 512:
                                                 rs * 1024 + mo * 512 + 512],
                                          start=(fc == 0), stop=(fc == 31),
                                          after=aft if (sc3 == 0 and mo == 0) else [])
                w2_user[rs] = mm_last[2]
            for sc3 in range(3):
                ev = vtt(eo_e[e_][:, sc3 * 1024:(sc3 + 1) * 1024],
                         pse[sc3][:, 0:1024], sb_b2bc, ALU.add,
                         after=[mm_last[2], evb2])
                eo_evs.append(ev)
            psb384_user = eo_evs[-3:]
            P.mbar()
        P.barrier()
        arena.free("w1r"); arena.free("w2r")
        arena.free("b2bc")

        if "p6dbg" in dbg:
            P.dma("sync", dout("eo0", (128, 3 * 1024), BF16), eo_e[0],
                  after=eo_evs[0:3])
        sb_c8 = arena.alloc("c8", 8, F32)
        evagd = P.dma("sync", sb_c8[0:8, 0:8],
                      cc_cnt_out.ap().rearrange("(r e) o -> r (e o)", e=8),
                      after=[evcc2])
        evoff = mm(psA[0:1, 0:8], sb_ranksel[0:8, :], sb_c8[0:8, 0:8],
                   start=True, stop=True, after=[evagd])
        evoffs = scopy(sb_c8[0:1, 0:8], psA[0:1, 0:8], after=[evoff])
        evofc = tpose(psA[0:8, 8:9], sb_c8[0:1, 0:8], after=[evoffs], f32=True)
        evofc2 = scopy(sb_stat2[0:8, 4:5], psA[0:8, 8:9], after=[evofc])
        evk1 = vts(sb_lfr[0:8, :], sb_incl[0:8, :], sb_stat2[0:8, 4:5], None,
                   ALU.add, after=[evofc2, evlf2])
        evk2 = vts(sb_lfr[0:8, :], sb_lfr[0:8, :], 2048.5, None, ALU.is_le,
                   after=[evk1])
        evk3 = vtt(sb_lfr[0:8, :], sb_lfr[0:8, :], sb_mem[0:8, :], ALU.mult,
                   after=[evk2])
        kr = []
        for mo in range(4):
            kr.append(mm(psA[0:1, mo * 512:(mo + 1) * 512], sb_ones8[0:8, :],
                         sb_lfr[0:8, mo * 512:(mo + 1) * 512],
                         start=True, stop=True, after=[evk3]))
        evkr = scopy(kept_row, psA[0:1, 0:2048], after=kr)
        kg_evs = []
        for t in range(NT):
            et = tpose(psA[:, 8:9], kept_row[:, t * 128:(t + 1) * 128],
                       after=[evkr] + kg_evs[-1:], f32=True)
            ee = scopy(sb_stat2[:, 5:6], psA[:, 8:9], after=[et])
            ev = vtt(sb_kg[:, t:t + 1], sb_stat2[:, 5:6], sb_g1[:, t:t + 1],
                     ALU.mult, after=[ee])
            kg_evs.append(ev)

        if "p6dbg" in dbg:
            P.dma("sync", dout("kgd", (128, 16)), sb_kg, after=kg_evs)
            P.dma("sync", dout("offsd", (8, 1)), sb_stat2[0:8, 4:5],
                  after=[evofc2, evk1])
        lfb = []
        for mo in range(4):
            lfb.append(mm(psA[:, mo * 512:(mo + 1) * 512], sb_ones1[0:1, :],
                          sb_lfrow[0:1, mo * 512:(mo + 1) * 512],
                          start=True, stop=True, after=[evkr] + kg_evs))

        sb_Cp = arena.alloc("Cp", 2 * 128, BF16)
        cuser = [None, None]
        out_evs = []
        for t in range(NT):
            mm_last = None
            for sc in range(24):
                rb = sc % 2
                dep = [cuser[rb]] if cuser[rb] else []
                evc = vts(sb_Cp[:, rb * 128:(rb + 1) * 128],
                          psA[:, t * 128:(t + 1) * 128],
                          sb_slotid[:, sc:sc + 1], None, ALU.is_equal,
                          after=lfb + dep)
                for mo in range(2):
                    aft = [evc]
                    if sc == 0 and mo == 0:
                        aft = aft + eo_evs + out_evs[-1:]
                    mm_last = mm(psB[:, mo * 512:(mo + 1) * 512],
                                 sb_Cp[:, rb * 128:(rb + 1) * 128],
                                 eo_e[sc // 3][:, (sc % 3) * 1024 + mo * 512:
                                               (sc % 3) * 1024 + mo * 512 + 512],
                                 start=(sc == 0), stop=(sc == 23),
                                 after=aft if mo == 0 else [])
                cuser[rb] = mm_last
            evo = vstt(sb_stream[:, t * 1024:(t + 1) * 1024], psB[:, 0:1024],
                       sb_kg[:, t:t + 1],
                       sb_stream[:, t * 1024:(t + 1) * 1024],
                       ALU.mult, ALU.add, after=[mm_last] + kg_evs)
            out_evs.append(evo)
            P.dma("sync", out_h[t * 128:(t + 1) * 128, :],
                  sb_stream[:, t * 1024:(t + 1) * 1024], after=[evo])
            P.mbar()

        P.barrier()
        P.emit(block)
    return nc, dbg_outs, P


_CACHE = {}


def _get_program(dbg=(), phase_limit=99, auto_bar=False):
    key = (tuple(sorted(dbg)), phase_limit, auto_bar)
    if key not in _CACHE:
        _CACHE[key] = build_program(dbg=dbg, phase_limit=phase_limit,
                                    auto_bar=auto_bar)
    return _CACHE[key]


def make_in_maps(inputs):
    X = np.ascontiguousarray(np.asarray(inputs["L0_attn_x"], dtype=np.float32))
    base = dict(
        Mq=np.ascontiguousarray(inputs["L0_attn_Mq"], dtype=np.float32),
        Mk=np.ascontiguousarray(inputs["L0_attn_Mk"], dtype=np.float32),
        Mv=np.ascontiguousarray(inputs["L0_attn_Mv"], dtype=np.float32),
        G1R=np.ascontiguousarray(np.asarray(inputs["L0_ffn_G1_R"],
                                            dtype=np.float32).reshape(32, 4, 32)),
        G2R=np.ascontiguousarray(np.asarray(inputs["L0_ffn_G2_R"],
                                            dtype=np.float32).reshape(4, 16, 16)),
        G1I=np.ascontiguousarray(np.asarray(inputs["L0_ffn_G1_I"],
                                            dtype=np.float32).reshape(32, 4, 32)),
        G2I=np.ascontiguousarray(np.asarray(inputs["L0_ffn_G2_I"],
                                            dtype=np.float32).reshape(4, 16, 16)),
        bR=np.ascontiguousarray(inputs["L0_ffn_bias_real"], dtype=np.float32),
        bI=np.ascontiguousarray(inputs["L0_ffn_bias_imag"], dtype=np.float32),
        W_in=np.ascontiguousarray(inputs["L1_attn_W_in"], dtype=np.float32),
        conv_w=np.ascontiguousarray(np.asarray(inputs["L1_attn_conv_w"],
                                               dtype=np.float32).reshape(2048, 4)),
        ln_g=np.ascontiguousarray(inputs["L1_attn_ln_gamma"], dtype=np.float32),
        ln_b=np.ascontiguousarray(inputs["L1_attn_ln_beta"], dtype=np.float32),
        W_xp=np.ascontiguousarray(inputs["L1_attn_W_xproj"], dtype=np.float32),
        A_p=np.ascontiguousarray(inputs["L1_attn_A_param"], dtype=np.float32),
        Dvec=np.ascontiguousarray(inputs["L1_attn_D"], dtype=np.float32),
        W_out=np.ascontiguousarray(inputs["L1_attn_W_out"], dtype=np.float32),
        wg=np.ascontiguousarray(inputs["L1_ffn_wg"], dtype=np.float32),
        w1=np.ascontiguousarray(inputs["L1_ffn_w1"], dtype=np.float32),
        b1=np.ascontiguousarray(inputs["L1_ffn_b1"], dtype=np.float32),
        w2=np.ascontiguousarray(inputs["L1_ffn_w2"], dtype=np.float32),
        b2=np.ascontiguousarray(inputs["L1_ffn_b2"], dtype=np.float32),
    )
    in_maps = []
    for core in range(8):
        b, half = core // 2, core % 2
        m = dict(base)
        m["x_full"] = np.ascontiguousarray(X[b])
        m["x_own"] = np.ascontiguousarray(X[b][half * T:(half + 1) * T])
        m["rank_sel"] = (np.arange(8) < core).astype(np.float32).reshape(8, 1)
        hs = np.zeros((24, 3), dtype=np.float32)
        if half == 1:
            partner = core - 1
            for j in range(3):
                hs[partner * 3 + j, j] = 1.0
        m["halo_sel24"] = hs
        in_maps.append(m)
    return in_maps


def kernel(**inputs):
    nc, dbg_outs, P = _get_program()
    in_maps = make_in_maps(inputs)
    res = run_bass_kernel_spmd(nc, in_maps, core_ids=list(range(8)))
    out = np.zeros((4, S, D), dtype=np.float32)
    for core in range(8):
        b, half = core // 2, core % 2
        out[b, half * T:(half + 1) * T] = np.asarray(res.results[core]["out"])
    return out


# version 02 (final)
# speedup vs baseline: 1.1135x; 1.1135x over previous
import numpy as np
import concourse.bass as bass
import concourse.mybir as mybir
from concourse.bass_utils import run_bass_kernel_spmd

F32 = mybir.dt.float32
BF16 = mybir.dt.bfloat16
AF = mybir.ActivationFunctionType
ALU = mybir.AluOpType
AX = mybir.AxisListType

D = 1024
S = 4096
T = 2048
NT = 16
EXPERTS = 8
CAP = 2048
CAPL = 512
SLOTS = EXPERTS * CAPL
EPS = 1e-5

QUEUES = ("tensor", "vector", "scalar", "gpsimd", "sync")


class Prog:

    KD = 6

    def __init__(self, nc, auto_bar=False, sim_safe=True):
        self.nc = nc
        self.sim_safe = sim_safe
        self.sems = {}
        self.count = {q: 0 for q in QUEUES}
        self.dcount = {q: 0 for q in QUEUES}
        self.items = {q: [] for q in QUEUES}
        self.waited = {q: {} for q in QUEUES}
        self.n_instr = 0
        self.auto_bar = auto_bar

    def ev(self, q, dma=True):
        evs = [("c", q, self.count[q])]
        if dma:
            evs += self.dma_evs(q)
        return ("multi", evs)

    def dma_evs(self, q):
        out = []
        for k in range(self.KD):
            n = (self.dcount[q] + self.KD - 1 - k) // self.KD
            if n > 0:
                out.append(("d", q, k, 16 * n))
        return out

    def _expand(self, after):
        flat = []
        for ev in after:
            if ev is None:
                continue
            if ev[0] == "multi":
                flat.extend(ev[1])
            else:
                flat.append(ev)
        return flat

    def _resolve(self, q, after):
        waits = []
        for ev in self._expand(after):
            if ev[0] == "c":
                _, q2, v = ev
                if v <= 0 or (q2 == q and not self.sim_safe):
                    continue
                key = ("c", q2)
            else:
                _, q2, k, v = ev
                if v <= 0:
                    continue
                key = ("d", q2, k)
            if v > self.waited[q].get(key, 0):
                self.waited[q][key] = v
                waits.append((key, v))
        return waits

    def op(self, q, fn, after=(), inc=1):
        waits = self._resolve(q, after)
        self.items[q].append((fn, waits, ("c", q), inc))
        self.count[q] += inc
        self.n_instr += 1 + len(waits)
        return ("c", q, self.count[q])

    def dma(self, q, out, in_, after=(), slow=False):
        i = self.dcount[q]
        k = i % self.KD
        key = ("d", q, k)
        prior = 16 * (i // self.KD)
        waits = self._resolve(q, list(after) + [("d", q, k, prior)])
        if prior > 0 and (key, prior) not in waits                 and self.waited[q].get(key, 0) >= prior:
            pass
        def fn(e):
            return e.dma_start(out=out, in_=in_,
                               allow_slow_non_contiguous=slow)
        self.items[q].append((fn, waits, key, 16))
        self.dcount[q] += 1
        self.n_instr += 1 + len(waits)
        return ("d", q, k, prior + 16)

    def barrier(self):
        evs = []
        for q in QUEUES:
            evs.append(("c", q, self.count[q]))
            evs.extend(self.dma_evs(q))
        for q in QUEUES:
            waits = self._resolve(q, evs)
            if waits:
                self.items[q].append((None, waits, ("c", q), 1))
                self.count[q] += 1
                self.n_instr += 1 + len(waits)
        return [self.ev(q) for q in QUEUES]

    def mbar(self):
        if self.auto_bar:
            self.barrier()

    KR = 4

    def make_sems(self, nc, stack):
        for q in QUEUES:
            for r in range(self.KR):
                self.sems[("c", q, r)] = stack.enter_context(
                    nc.semaphore(f"s_{q}_{r}"))
            for k in range(self.KD):
                self.sems[("d", q, k)] = stack.enter_context(
                    nc.semaphore(f"s_{q}_d{k}"))

    def emit(self, block):
        waited = {}
        for q in QUEUES:
            for (_fn, waits, _k, _i) in self.items[q]:
                for (key, v) in waits:
                    waited.setdefault(key, set()).add(v)
        thr = {q: sorted(waited.get(("c", q), set())) for q in QUEUES}
        tmap = {q: {v: m for m, v in enumerate(thr[q])} for q in QUEUES}

        def cwait(e, q2, v):
            m = tmap[q2][v]
            e.wait_ge(self.sems[("c", q2, m % self.KR)], m // self.KR + 1)

        def make_body(q, lst):
            comp_idx = [i for i, it in enumerate(lst) if it[2] == ("c", q)]
            last_comp = comp_idx[-1] if comp_idx else -1
            wset = set(thr[q])

            def body(e):
                running = 0
                for i, (fn, waits, inckey, inc) in enumerate(lst):
                    for (key, v) in waits:
                        if key[0] == "c":
                            cwait(e, key[1], v)
                        else:
                            e.wait_ge(self.sems[key], v)
                    if inckey == ("c", q):
                        running += inc
                        ins = e.nop() if fn is None else fn(e)
                        if running in wset:
                            m = tmap[q][running]
                            ins.then_inc(self.sems[("c", q, m % self.KR)], 1)
                    else:
                        fn(e).then_inc(self.sems[inckey], inc)
            return body
        for q in QUEUES:
            getattr(block, q)(make_body(q, self.items[q]))


class Arena:

    def __init__(self, nc, name, cols_bf16):
        self.t = nc.alloc_sbuf_tensor(name, [128, cols_bf16], BF16)
        self.free_list = [(0, cols_bf16)]
        self.live = {}

    def alloc(self, name, cols, dtype):
        need = cols * (2 if dtype == F32 else 1)
        need = (need + 15) & ~15
        for i, (off, sz) in enumerate(self.free_list):
            if sz >= need:
                if sz == need:
                    del self.free_list[i]
                else:
                    self.free_list[i] = (off + need, sz - need)
                self.live[name] = (off, need)
                ap = self.t.ap()[:, off:off + need]
                if dtype == F32:
                    ap = ap.bitcast(F32)[:, 0:cols]
                else:
                    ap = ap[:, 0:cols]
                return ap
        raise MemoryError(f"arena: cannot alloc {name} ({need} cols); "
                          f"free={self.free_list} live={sorted(self.live)}")

    def sub(self, parent, bf16_off, cols, dtype):
        off, sz = self.live[parent]
        need = cols * (2 if dtype == F32 else 1)
        assert bf16_off + need <= sz, (parent, bf16_off, need, sz)
        ap = self.t.ap()[:, off + bf16_off:off + bf16_off + need]
        if dtype == F32:
            ap = ap.bitcast(F32)[:, 0:cols]
        else:
            ap = ap[:, 0:cols]
        return ap

    def free(self, name):
        off, sz = self.live.pop(name)
        self.free_list.append((off, sz))
        self.free_list.sort()
        merged = []
        for off, sz in self.free_list:
            if merged and merged[-1][0] + merged[-1][1] == off:
                merged[-1] = (merged[-1][0], merged[-1][1] + sz)
            else:
                merged.append((off, sz))
        self.free_list = merged


def build_program(dbg=(), phase_limit=99, auto_bar=False, p1_windows=12,
                  sim_safe=True, p1_mode=0):
    nc = bass.Bass("TRN2", target_bir_lowering=False)

    def din(name, shape, dtype=F32):
        return nc.dram_tensor(name, list(shape), dtype, kind="ExternalInput").ap()

    x_full_h = din("x_full", (S, D))
    x_own_h = din("x_own", (T, D))
    Mq_h = din("Mq", (D, 512)); Mk_h = din("Mk", (D, 512)); Mv_h = din("Mv", (D, D))
    G1R_h = din("G1R", (32, 4, 32)); G2R_h = din("G2R", (4, 16, 16))
    G1I_h = din("G1I", (32, 4, 32)); G2I_h = din("G2I", (4, 16, 16))
    bR_h = din("bR", (512,)); bI_h = din("bI", (512,))
    Win_h = din("W_in", (D, 4096)); convw_h = din("conv_w", (2048, 4))
    lng_h = din("ln_g", (2048,)); lnbh_h = din("ln_b", (2048,))
    Wxp_h = din("W_xp", (2048, 33)); Ap_h = din("A_p", (2048, 16))
    Dv_h = din("Dvec", (2048,)); Wout_h = din("W_out", (2048, D))
    wg_h = din("wg", (D, 8))
    w1_h = din("w1", (EXPERTS, D, 4096)); b1_h = din("b1", (EXPERTS, 4096))
    w2_h = din("w2", (EXPERTS, 4096, D)); b2_h = din("b2", (EXPERTS, D))
    ranksel_h = din("rank_sel", (8, 1))
    halosel_h = din("halo_sel24", (24, 3))

    out_h = nc.dram_tensor("out", [T, D], F32, kind="ExternalOutput").ap()

    dbg_outs = {}

    def dout(name, shape, dtype=F32):
        if name not in dbg_outs:
            dbg_outs[name] = nc.dram_tensor("dbg_" + name, list(shape), dtype,
                                            kind="ExternalOutput").ap()
        return dbg_outs[name]

    cc_halo_in = nc.dram_tensor("cc_halo_in", [3, D], F32, kind="Internal",
                                addr_space="Local")
    cc_halo_out = nc.dram_tensor("cc_halo_out", [24, D], F32, kind="Internal",
                                 addr_space="Shared")
    cc_cnt_in = nc.dram_tensor("cc_cnt_in", [8, 1], F32, kind="Internal",
                               addr_space="Local")
    cc_cnt_out = nc.dram_tensor("cc_cnt_out", [64, 1], F32, kind="Internal",
                                addr_space="Shared")

    c_identf = nc.inline_tensor(np.eye(128, dtype=np.float32), name="c_identf")
    c_ones1 = nc.inline_tensor(np.ones((1, 128), dtype=np.float32), name="c_ones1")
    c_iota512 = nc.inline_tensor(
        np.broadcast_to(np.arange(512, dtype=np.float32), (128, 512)).copy(),
        name="c_iota512")
    c_ones8 = nc.inline_tensor(np.ones((8, 1), dtype=np.float32), name="c_ones8")
    c_lnscale = nc.inline_tensor(
        np.full((128, 1), 1.0 / 2048.0, dtype=np.float32), name="c_lnscale")
    c_ebase = nc.inline_tensor(
        (np.arange(8, dtype=np.float32) * CAPL - 1.0).reshape(8, 1).copy(),
        name="c_ebase")
    c_slotid = nc.inline_tensor(
        (np.arange(128, dtype=np.float32)[:, None]
         + 128.0 * np.arange(SLOTS // 128, dtype=np.float32)[None, :]).copy(),
        name="c_slotid")
    seln = np.zeros((16, 16 * 128), dtype=np.float32)
    for n in range(16):
        seln[n, n * 128:(n + 1) * 128] = 1.0
    c_sel16 = nc.inline_tensor(seln, name="c_sel16")

    arena = Arena(nc, "arena", ((nc.sbuf_bytes_remaining - 512) // 2) & ~15)
    P = Prog(nc, auto_bar=auto_bar, sim_safe=sim_safe)

    from contextlib import ExitStack
    stack = ExitStack()
    with (
        nc.psum_tensor("psA", [128, 2048], F32) as psA,
        nc.psum_tensor("psB", [128, 2048], F32) as psB,
        stack,
        nc.Block() as block,
    ):
        P.make_sems(nc, stack)

        sb_identf = arena.alloc("identf", 128, F32)
        sb_identb = arena.alloc("identb", 128, BF16)
        sb_ones1 = arena.alloc("ones1", 128, F32)
        sb_ones1b = arena.alloc("ones1b", 128, BF16)
        sb_iota512 = arena.alloc("iota512", 512, F32)
        sb_ones8 = arena.alloc("ones8", 1, F32)
        sb_lnsc = arena.alloc("lnsc", 1, BF16)
        sb_ebase = arena.alloc("ebase", 1, F32)
        sb_slotid = arena.alloc("slotid", SLOTS // 128, F32)
        sb_ranksel = arena.alloc("ranksel", 1, F32)
        sb_halosel = arena.alloc("halosel", 3, BF16)

        P.dma("sync", sb_identf, c_identf.ap())
        P.dma("gpsimd", sb_identb, c_identf.ap())
        P.dma("sync", sb_ones1[0:1, :], c_ones1.ap())
        P.dma("gpsimd", sb_ones1b[0:1, :], c_ones1.ap())
        P.dma("sync", sb_iota512, c_iota512.ap())
        P.dma("sync", sb_ones8[0:8, :], c_ones8.ap())
        P.dma("gpsimd", sb_lnsc, c_lnscale.ap())
        P.dma("sync", sb_ebase[0:8, :], c_ebase.ap())
        P.dma("sync", sb_slotid, c_slotid.ap())
        P.dma("sync", sb_ranksel[0:8, :], ranksel_h)
        P.dma("gpsimd", sb_halosel[0:24, :], halosel_h)

        sb_stream = arena.alloc("stream", NT * 1024, F32)
        sb_eps = arena.alloc("eps", 1, F32)

        P.op("vector", lambda e: e.memset(sb_eps, EPS))

        def mm(ps_ap, lhsT, rhs, start, stop, after=()):
            return P.op("tensor", lambda e: e.matmul(
                ps_ap, lhsT, rhs, start=start, stop=stop), after=after)

        def tpose(ps_ap, in_ap, after=(), f32=True):
            p = in_ap.shape[0]
            ident = sb_identf[0:p, 0:p]
            return P.op("tensor", lambda e: e.transpose(ps_ap, in_ap, ident),
                        after=after)

        def act(q, out, in_, func, bias=0.0, scale=1.0, accum=None, after=()):
            return P.op(q, lambda e: e.activation(
                out, in_, func, bias=bias, scale=scale, accum_out=accum),
                after=after)

        def vcopy(out, in_, after=()):
            return P.op("vector", lambda e: e.tensor_copy(out, in_), after=after)

        def scopy(out, in_, after=()):
            return P.op("scalar", lambda e: e.copy(out, in_), after=after)

        def vtt(out, a, b, alu, after=()):
            return P.op("vector", lambda e: e.tensor_tensor(out, a, b, alu),
                        after=after)

        def vadd(out, a, b, after=()):
            return P.op("vector", lambda e: e.tensor_add(out, a, b), after=after)

        def vts(out, in0, s1, s2, op0, op1=None, after=()):
            if op1 is None:
                return P.op("vector", lambda e: e.tensor_scalar(
                    out, in0, s1, None, op0=op0), after=after)
            return P.op("vector", lambda e: e.tensor_scalar(
                out, in0, s1, s2, op0=op0, op1=op1), after=after)

        def vstt(out, in0, scalar, in1, op0, op1, after=()):
            return P.op("vector", lambda e: e.scalar_tensor_tensor(
                out, in0, scalar, in1, op0=op0, op1=op1), after=after)

        def ln_tile(x_ap, out_ap, stat, after=()):
            n = x_ap.shape[1]
            p = x_ap.shape[0]
            st = stat[0:p, :]
            e1 = P.op("vector", lambda e: e.bn_stats(st[:, 0:6], x_ap[:, 0:n // 2]),
                      after=after)
            e1b = P.op("vector", lambda e: e.bn_stats(st[:, 6:12], x_ap[:, n // 2:n]),
                       after=[e1])
            e2 = P.op("vector", lambda e: e.bn_aggr(st[:, 12:14], st[:, 0:12]),
                      after=[e1b])
            e3 = act("scalar", st[:, 14:15], st[:, 13:14], AF.Ln,
                     bias=sb_eps[0:p, 0:1], after=[e2])
            e4 = act("scalar", st[:, 14:15], st[:, 14:15], AF.Exp, scale=-0.5,
                     after=[e3])
            e5 = vts(out_ap, x_ap, st[:, 12:13], st[:, 14:15],
                     ALU.subtract, ALU.mult, after=[e4])
            return e5

        def softplus(out_ap, in_ap, tmp_ap, after=()):
            e1 = act("scalar", tmp_ap, in_ap, AF.Abs, after=after)
            e2 = act("scalar", tmp_ap, tmp_ap, AF.Exp, scale=-1.0, after=[e1])
            e3 = act("scalar", tmp_ap, tmp_ap, AF.Ln, bias=1.0, after=[e2])
            e4 = act("scalar", out_ap, in_ap, AF.Relu, after=[e3])
            e5 = vadd(out_ap, out_ap, tmp_ap, after=[e4])
            return e5

        tp_user = {0: None, 1: None}

        def tp_tile_to_fmajor(src_tile, dst, dst_cols, j, prev_evs, f32=False):
            stride = dst_cols(1) - dst_cols(0)
            evs = []
            for g in range(2):
                sl = g % 2
                base = 1024 + sl * 512
                dep = list(prev_evs)
                if tp_user[sl] is not None:
                    dep.append(tp_user[sl])
                ets = []
                for k in range(4):
                    fc = g * 4 + k
                    ets.append(tpose(psA[:, base + k * 128: base + k * 128 + 128],
                                     src_tile[:, fc * 128:(fc + 1) * 128],
                                     after=dep if k == 0 else [], f32=f32))
                dview = dst[:, dst_cols(g * 4) + j * 128:
                            dst_cols(g * 4) + j * 128 + 3 * stride + 128]                     .rearrange("p (c x) -> p c x", c=1)
                dview = bass.AP(dview.tensor, dview.offset,
                                [list(dview.ap[0]), [stride, 4], [1, 128]])
                ee = P.op("scalar", lambda e, dview=dview, base=base: e.copy(
                    dview, psA[:, base:base + 512]
                    .rearrange("p (c x) -> p c x", x=128)), after=[ets[-1]])
                tp_user[sl] = ee
                evs.append(ee)
            return evs

        sb_Mq = arena.alloc("Mq", 8 * 512, BF16)
        sb_Mk = arena.alloc("Mk", 8 * 512, BF16)
        sb_Mv = arena.alloc("Mv", 8 * 1024, BF16)
        for c in range(8):
            P.dma("gpsimd", sb_Mq[:, c * 512:(c + 1) * 512],
                  Mq_h[c * 128:(c + 1) * 128, :])
            P.dma("gpsimd", sb_Mk[:, c * 512:(c + 1) * 512],
                  Mk_h[c * 128:(c + 1) * 128, :])
            P.dma("gpsimd", sb_Mv[:, c * 1024:(c + 1) * 1024],
                  Mv_h[c * 128:(c + 1) * 128, :])

        sb_convw = arena.alloc("convw", 64, F32)
        P.dma("sync", sb_convw.rearrange("p (c j) -> p c j", j=4),
              convw_h.rearrange("(c p) j -> p c j", p=128), slow=True)
        sb_lng = arena.alloc("lng", 16, F32)
        sb_lnbeta = arena.alloc("lnbeta", 16, F32)
        P.dma("sync", sb_lng, lng_h.rearrange("(c p) -> p c", p=128), slow=True)
        P.dma("sync", sb_lnbeta, lnbh_h.rearrange("(c p) -> p c", p=128), slow=True)
        sb_Dv = arena.alloc("Dv", 16, F32)
        P.dma("sync", sb_Dv, Dv_h.rearrange("(c p) -> p c", p=128), slow=True)
        sb_Wxp = arena.alloc("Wxp", 16 * 33, BF16)
        for c in range(16):
            P.dma("gpsimd", sb_Wxp[:, c * 33:(c + 1) * 33],
                  Wxp_h[c * 128:(c + 1) * 128, :])
        sb_wg = arena.alloc("wg", 8 * 8, F32)
        for c in range(8):
            P.dma("sync", sb_wg[:, c * 8:(c + 1) * 8],
                  wg_h[c * 128:(c + 1) * 128, :])
        sb_A = arena.alloc("A", 256, F32)
        sb_Atmp = arena.sub("stream", 11264, 256, F32)
        evA = P.dma("sync", sb_A.rearrange("p (c n) -> p c n", n=16),
                    Ap_h.rearrange("(c p) n -> p c n", p=128), slow=True)
        evA = softplus(sb_A, sb_A, sb_Atmp, after=[evA])
        evA = vts(sb_A, sb_A, 0.1, 10.0, ALU.max, ALU.min, after=[evA])
        evA = vts(sb_A, sb_A, -1.0, None, ALU.mult, after=[evA])
        sb_b1 = arena.alloc("b1", 8 * 32, F32)
        for e_ in range(EXPERTS):
            P.dma("sync", sb_b1[:, e_ * 32:(e_ + 1) * 32],
                  b1_h[e_].rearrange("(c p) -> p c", p=128), slow=True)
        sb_b2 = arena.alloc("b2", 1024, BF16)
        for e_ in range(EXPERTS):
            P.dma("gpsimd", sb_b2[e_:e_ + 1, 0:1024],
                  b2_h[e_].rearrange("(o m) -> o m", o=1))
        ld0 = [P.ev("gpsimd"), P.ev("sync"), evA]
        P.mbar()
        if phase_limit <= 1:
            P.barrier(); P.emit(block)
            return nc, dbg_outs, P

        sb_qm = arena.alloc("qm", 4 * 2048, BF16)
        sb_km = arena.alloc("km", 4 * 2048, BF16)
        sb_vp = arena.alloc("vp", 64 * 512, BF16)
        sb_xld = arena.sub("stream", 0, 2 * 1024, BF16)
        sb_lntm = arena.sub("stream", 2048, 2 * 1024, F32)
        sb_xlT = arena.sub("stream", 6144, 8 * 512, BF16)
        sb_vsc = arena.sub("stream", 10240, 512, F32)
        sb_stat = arena.alloc("stat", 16, F32)

        xld_user = {0: None, 1: None}
        lntm_user = {0: None, 1: None}

        def window(src_h, w_src, do_v, vslot, do_qk, deps_in):
            if p1_mode >= 1:
                do_v = False
                do_qk = False
            tp_evs = []
            for j in range(4):
                rb = j % 2
                xt = sb_xld[:, rb * 1024:(rb + 1) * 1024]
                lt = sb_lntm[:, rb * 1024:(rb + 1) * 1024]
                dep = list(deps_in)
                if xld_user[rb] is not None:
                    dep.append(xld_user[rb])
                evl = P.dma("gpsimd", xt,
                            src_h[(w_src * 4 + j) * 128:(w_src * 4 + j + 1) * 128, :],
                            after=dep)
                dep2 = [evl]
                if lntm_user[rb] is not None:
                    dep2.append(lntm_user[rb])
                evn = ln_tile(xt, lt, sb_stat, after=dep2)
                xld_user[rb] = evn
                if p1_mode >= 2:
                    tp_evs.append(evn)
                else:
                    evs = tp_tile_to_fmajor(lt, sb_xlT, lambda fc: fc * 512, j, [evn])
                    lntm_user[rb] = evs[-1]
                    tp_evs += evs
                P.mbar()
            deps = tp_evs
            if do_v:
                for clo in range(8):
                    mmevs = []
                    for ifc in range(8):
                        mmevs.append(mm(psB[:, 0:512],
                                        sb_Mv[:, ifc * 1024 + clo * 128:
                                              ifc * 1024 + clo * 128 + 128],
                                        sb_xlT[:, ifc * 512:(ifc + 1) * 512],
                                        start=(ifc == 0), stop=(ifc == 7),
                                        after=deps if ifc == 0 else []))
                    eve = scopy(sb_vsc, psB[:, 0:512], after=[mmevs[-1]])
                    tevs = []
                    for ttc in range(4):
                        tevs.append(tpose(psB[:, 512 + ttc * 128:512 + ttc * 128 + 128],
                                          sb_vsc[:, ttc * 128:(ttc + 1) * 128],
                                          after=[eve]))
                    evv = scopy(sb_vp[:, (vslot * 8 + clo) * 512:
                                      (vslot * 8 + clo) * 512 + 512],
                                psB[:, 512:1024], after=tevs)
                    deps = [evv]
                    P.mbar()
            if do_qk:
                for (msb, dst) in ((sb_Mq, sb_qm), (sb_Mk, sb_km)):
                    for oc in range(4):
                        mmevs = []
                        for ifc in range(8):
                            mmevs.append(mm(
                                psB[:, 1024:1536],
                                msb[:, ifc * 512 + oc * 128:ifc * 512 + oc * 128 + 128],
                                sb_xlT[:, ifc * 512:(ifc + 1) * 512],
                                start=(ifc == 0), stop=(ifc == 7),
                                after=deps if ifc == 0 else []))
                        ev = scopy(dst[:, oc * 2048 + w_src * 512:
                                       oc * 2048 + w_src * 512 + 512],
                                   psB[:, 1024:1536], after=[mmevs[-1]])
                        deps = [ev]
                        P.mbar()
            return deps

        prev = ld0
        for w in range(min(8, p1_windows)):
            prev = window(x_full_h, w, True, w, False, prev)
        for w in range(max(0, min(4, p1_windows - 8))):
            prev = window(x_own_h, w, False, None, True, prev)

        P.barrier()
        for nm in ("Mq", "Mk", "Mv"):
            arena.free(nm)

        if "qm" in dbg:
            o = dout("qm", (512, T), BF16)
            for oc in range(4):
                P.dma("sync", o[oc * 128:(oc + 1) * 128, :],
                      sb_qm[:, oc * 2048:(oc + 1) * 2048])
        if "km" in dbg:
            o = dout("km", (512, T), BF16)
            for oc in range(4):
                P.dma("sync", o[oc * 128:(oc + 1) * 128, :],
                      sb_km[:, oc * 2048:(oc + 1) * 2048])
        if "vp" in dbg:
            P.dma("sync", dout("vp", (128, 64 * 512), BF16), sb_vp)
        if phase_limit <= 2:
            P.barrier(); P.emit(block)
            return nc, dbg_outs, P

        sb_Kp = arena.alloc("Kp", 4096, BF16)
        sb_Qp = arena.alloc("Qp", 4096, BF16)
        sb_E = arena.alloc("E", 2048, BF16)
        sb_colsum = arena.alloc("colsum", 64, F32)
        sb_lnbias = arena.alloc("lnbias", 32, F32)
        sb_ao = arena.alloc("ao", 4 * 512, F32)
        sb_xres = arena.alloc("xres", 2 * 512, F32)

        res_user = {0: None, 1: None}
        head_end = []
        for g in range(4):
            evKQ = []
            for cb in range(8):
                oc, pofs = cb // 2, (cb % 2) * 64
                evKQ.append(P.dma(
                    "sync", sb_Kp[0:64, cb * 512:(cb + 1) * 512],
                    sb_km[pofs:pofs + 64, oc * 2048 + g * 512:oc * 2048 + g * 512 + 512],
                    after=head_end))
                evKQ.append(P.dma(
                    "sync", sb_Qp[0:64, cb * 512:(cb + 1) * 512],
                    sb_qm[pofs:pofs + 64, oc * 2048 + g * 512:oc * 2048 + g * 512 + 512],
                    after=head_end))

            prev1 = evKQ
            for c2 in range(32):
                lhsT = sb_Kp[0:64, c2 * 128:(c2 + 1) * 128]
                for hh in range(2):
                    mmevs = []
                    for cb in range(4):
                        rhs = sb_Qp[0:64, (hh * 4 + cb) * 512:(hh * 4 + cb + 1) * 512]
                        mmevs.append(mm(psA[:, cb * 512:(cb + 1) * 512], lhsT, rhs,
                                        start=True, stop=True,
                                        after=prev1 if cb == 0 else []))
                    eva = act("scalar", sb_E, psA[:, 0:2048], AF.Exp, scale=0.125,
                              accum=sb_colsum[:, c2 * 2 + hh:c2 * 2 + hh + 1],
                              after=[mmevs[-1]])
                    prev1 = [eva]
                    P.mbar()
            cs = sb_colsum.rearrange("p (c two) -> p c two", two=2)
            ev1 = vadd(sb_lnbias[:, 0:32], cs[:, :, 0], cs[:, :, 1], after=prev1)
            ev2 = act("scalar", sb_lnbias[:, 0:32], sb_lnbias[:, 0:32], AF.Ln,
                      after=[ev1])
            ev3 = vts(sb_lnbias[:, 0:32], sb_lnbias[:, 0:32], -1.0, None,
                      ALU.mult, after=[ev2])

            prev2 = [ev3]
            head_end = []
            for grp in range(2):
                for c2 in range(32):
                    clo, ttc = c2 // 4, c2 % 4
                    lhsT = sb_Kp[0:64, c2 * 128:(c2 + 1) * 128]
                    mmevs = []
                    for cb in range(4):
                        h2 = grp * 4 + cb
                        rhs = sb_Qp[0:64, h2 * 512:(h2 + 1) * 512]
                        mmevs.append(mm(psA[:, cb * 512:(cb + 1) * 512], lhsT, rhs,
                                        start=True, stop=True,
                                        after=prev2 if cb == 0 else []))
                    eva = act("scalar", sb_E, psA[:, 0:2048], AF.Exp, scale=0.125,
                              bias=sb_lnbias[:, c2:c2 + 1], after=[mmevs[-1]])
                    oevs = []
                    for cb in range(4):
                        h2 = grp * 4 + cb
                        lhsT2 = sb_vp[:, (h2 * 8 + clo) * 512 + ttc * 128:
                                      (h2 * 8 + clo) * 512 + ttc * 128 + 128]
                        oevs.append(mm(psB[:, cb * 512:(cb + 1) * 512], lhsT2,
                                       sb_E[:, cb * 512:(cb + 1) * 512],
                                       start=(c2 == 0), stop=(c2 == 31),
                                       after=[eva]))
                    prev2 = [oevs[-1]]
                    P.mbar()
                evacs = []
                for cb in range(4):
                    evacs.append(scopy(sb_ao[:, cb * 512:(cb + 1) * 512],
                                       psB[:, cb * 512:(cb + 1) * 512], after=prev2))
                frange = grp * 512
                res_evs = []
                for ts_ in range(4):
                    tile = g * 4 + ts_
                    rb = ts_ % 2
                    dep = [res_user[rb]] if res_user[rb] else []
                    evl = P.dma("sync", sb_xres[:, rb * 512:(rb + 1) * 512],
                                x_own_h[tile * 128:(tile + 1) * 128,
                                        frange:frange + 512], after=dep)
                    tevs = []
                    for cb in range(4):
                        tevs.append(tpose(psA[:, ts_ * 512 + cb * 128:
                                              ts_ * 512 + cb * 128 + 128],
                                          sb_ao[:, cb * 512 + ts_ * 128:
                                                cb * 512 + ts_ * 128 + 128],
                                          after=[evacs[cb]] + prev2))
                    evr = vadd(sb_stream[:, tile * 1024 + frange:
                                         tile * 1024 + frange + 512],
                               psA[:, ts_ * 512:ts_ * 512 + 512],
                               sb_xres[:, rb * 512:(rb + 1) * 512],
                               after=tevs + [evl])
                    res_user[rb] = evr
                    res_evs.append(evr)
                    P.mbar()
                prev2 = res_evs
                head_end = res_evs

        P.barrier()
        for nm in ("Kp", "Qp", "E", "colsum", "lnbias", "ao", "xres", "qm", "km", "vp"):
            arena.free(nm)

        if "x1" in dbg:
            o = dout("x1", (T, D))
            for t in range(NT):
                P.dma("sync", o[t * 128:(t + 1) * 128, :],
                      sb_stream[:, t * 1024:(t + 1) * 1024])
        if phase_limit <= 3:
            P.barrier(); P.emit(block)
            return nc, dbg_outs, P

        sb_G1X = arena.alloc("G1X", 4 * 32, F32)
        sb_G2X = arena.alloc("G2X", 16, F32)
        sb_Wacc = arena.alloc("Wacc", 4 * 512, F32)
        sb_Wtmp = arena.alloc("Wtmp", 512, F32)
        sb_WTr = arena.alloc("WTr", 4 * 512, BF16)
        sb_WTi = arena.alloc("WTi", 4 * 512, BF16)
        sb_WTin = arena.alloc("WTin", 4 * 512, BF16)

        def build_WT(g1h, g2h, dstr, dst_neg, start_dep=()):
            accev = list(start_dep)
            for cc in range(4):
                g1rep = g1h[:, cc, :].rearrange("b d -> d b")
                evs = []
                for ch in range(4):
                    for dd in range(8):
                        srcap = g1rep[8 * ch + dd:8 * ch + dd + 1, :] \
                            .broadcast_to([16, 32])
                        evs.append(P.dma(
                            "sync", sb_G1X[dd * 16:(dd + 1) * 16,
                                           ch * 32:(ch + 1) * 32],
                            srcap, after=accev, slow=True))
                g2src = g2h[cc, :, :].rearrange("e g -> g e")
                for r in range(8):
                    evs.append(P.dma("sync", sb_G2X[r * 16:(r + 1) * 16, 0:16],
                                     g2src, after=accev, slow=True))
                accev = []
                for ch in range(4):
                    in0 = sb_G1X[:, ch * 32:(ch + 1) * 32] \
                        .rearrange("p (b o) -> p b o", o=1).broadcast_to([128, 32, 16])
                    in1 = sb_G2X.rearrange("p (o e) -> p o e", o=1) \
                        .broadcast_to([128, 32, 16])
                    if cc == 0:
                        ev = P.op("vector", lambda e, ch=ch, in0=in0, in1=in1:
                                  e.tensor_tensor(
                                      sb_Wacc[:, ch * 512:(ch + 1) * 512]
                                      .rearrange("p (b e) -> p b e", b=32),
                                      in0, in1, ALU.mult), after=evs + accev)
                    else:
                        ev = P.op("vector", lambda e, in0=in0, in1=in1:
                                  e.tensor_tensor(
                                      sb_Wtmp.rearrange("p (b e) -> p b e", b=32),
                                      in0, in1, ALU.mult), after=evs + accev)
                        ev = vadd(sb_Wacc[:, ch * 512:(ch + 1) * 512],
                                  sb_Wacc[:, ch * 512:(ch + 1) * 512], sb_Wtmp,
                                  after=[ev])
                    accev.append(ev)
            ev = vcopy(dstr, sb_Wacc, after=accev)
            if dst_neg is not None:
                ev = vts(dst_neg, sb_Wacc, -1.0, None, ALU.mult, after=[ev])
            return ev

        evWr = build_WT(G1R_h, G2R_h, sb_WTr, None)
        evWi = build_WT(G1I_h, G2I_h, sb_WTi, sb_WTin, start_dep=[evWr])

        sb_bRI = arena.alloc("bRI", 8, F32)
        for c in range(4):
            P.dma("sync", sb_bRI[:, c:c + 1],
                  bR_h[c * 128:(c + 1) * 128].rearrange("(p o) -> p o", o=1))
            P.dma("sync", sb_bRI[:, 4 + c:5 + c],
                  bI_h[c * 128:(c + 1) * 128].rearrange("(p o) -> p o", o=1))

        sb_ln2 = arena.alloc("ln2", 2 * 1024, F32)
        sb_x2T = arena.alloc("x2T", 8 * 512, BF16)
        sb_ffs = arena.alloc("ffs", 512, F32)

        prevW = [evWr, evWi, P.ev("sync")]
        ln2_user = {0: None, 1: None}
        for chk in range(4):
            tp_evs = []
            for j in range(4):
                tile = chk * 4 + j
                rb = j % 2
                lt = sb_ln2[:, rb * 1024:(rb + 1) * 1024]
                dep = ([ln2_user[rb]] if ln2_user[rb] else []) + prevW
                evn = ln_tile(sb_stream[:, tile * 1024:(tile + 1) * 1024], lt,
                              sb_stat, after=dep)
                evs = tp_tile_to_fmajor(lt, sb_x2T, lambda fc: fc * 512, j, [evn])
                ln2_user[rb] = evs[-1]
                tp_evs += evs
                P.mbar()
            prevW = []
            psB_free = tp_evs
            for oc in range(8):
                is_im = oc >= 4
                ocl = oc % 4
                mmevs = []
                for k in range(8):
                    dgc = k % 4
                    from_im = k >= 4
                    wsb = (sb_WTin if (not is_im and from_im) else
                           sb_WTr if (not is_im or from_im) else sb_WTi)
                    lhsT = wsb[:, dgc * 512 + ocl * 128: dgc * 512 + ocl * 128 + 128]
                    rhs = sb_x2T[:, (4 * 512 if from_im else 0) + dgc * 512:
                                 (4 * 512 if from_im else 0) + dgc * 512 + 512]
                    mmevs.append(mm(psB[:, 0:512], lhsT, rhs,
                                    start=(k == 0), stop=(k == 7),
                                    after=psB_free if k == 0 else []))
                eve = act("scalar", sb_ffs, psB[:, 0:512], AF.Identity,
                          bias=sb_bRI[:, oc:oc + 1], after=[mmevs[-1]])
                psB_free = [eve]
                radd = []
                for ts_ in range(4):
                    tile = chk * 4 + ts_
                    sl = ts_ % 2
                    pslice = psA[:, sl * 128:(sl + 1) * 128]
                    dep = [eve] + (radd[-2:-1] if len(radd) >= 2 else [])
                    et = tpose(pslice, sb_ffs[:, ts_ * 128:(ts_ + 1) * 128],
                               after=dep)
                    fcol = oc * 128
                    evr = vadd(sb_stream[:, tile * 1024 + fcol:
                                         tile * 1024 + fcol + 128],
                               pslice,
                               sb_stream[:, tile * 1024 + fcol:
                                         tile * 1024 + fcol + 128],
                               after=[et])
                    radd.append(evr)
                psB_free = psB_free + radd
                prevW = radd
                P.mbar()

        P.barrier()
        for nm in ("G1X", "G2X", "Wacc", "Wtmp", "WTr", "WTi", "WTin", "bRI",
                   "ln2", "x2T", "ffs"):
            arena.free(nm)

        if "x2" in dbg:
            o = dout("x2", (T, D))
            for t in range(NT):
                P.dma("sync", o[t * 128:(t + 1) * 128, :],
                      sb_stream[:, t * 1024:(t + 1) * 1024])
        if phase_limit <= 4:
            P.barrier(); P.emit(block)
            return nc, dbg_outs, P

        ev = P.dma("sync", cc_halo_in.ap(), sb_stream[125:128, 15 * 1024:16 * 1024])
        evcc = P.op("gpsimd", lambda e: e.collective_compute(
            "AllGather", ALU.bypass, ins=[cc_halo_in.ap()],
            outs=[cc_halo_out.ap()], replica_groups=[[0, 1, 2, 3, 4, 5, 6, 7]]),
            after=[ev])

        sb_sel16 = arena.alloc("sel16", 16 * 128, BF16)
        evsel = P.dma("gpsimd", sb_sel16[0:16, :], c_sel16.ap())
        sb_Woutr = arena.alloc("Woutr", 3 * 1024, BF16)
        sb_Winr = arena.alloc("Winr", 3 * 1024, BF16)
        sb_x3T = arena.alloc("x3T", 8 * 512, BF16)
        sb_xcn = arena.alloc("xcn", 16 * 512, BF16)
        sb_zs = arena.alloc("zs", 16 * 512, BF16)
        sb_acc = arena.alloc("acc", 16 * 512, BF16)
        sb_accpad = arena.alloc("accpad", 2 * 512, F32)
        sb_y = arena.alloc("y", 16 * 512, BF16)
        sb_ln3 = arena.sub("y", 0, 2 * 1024, F32)
        sb_carry = arena.alloc("carry", 16 * 4, BF16)
        sb_xcp = arena.alloc("xcp", 516, BF16)
        sb_csc4 = arena.alloc("csc", 4 * 512, BF16)
        sb_csc = sb_csc4[:, 0:512]
        sb_sq = arena.alloc("sq", 512, BF16)
        sb_xph = arena.alloc("xph", 512, F32)
        sb_mrow = arena.alloc("mrow", 512, F32)
        sb_vrow = arena.alloc("vrow", 512, F32)
        sb_rrow = arena.alloc("rrow", 512, F32)
        sb_rowb = arena.alloc("rowb", 1024, BF16)
        sb_dbc = arena.alloc("dbc", 512, BF16)
        sb_bcf = arena.alloc("bcf", 2 * 512, F32)
        sb_bcb = arena.alloc("bcb", 512, BF16)
        sb_bcbs = arena.alloc("bcbs", 512, BF16)
        sb_term4 = arena.alloc("term", 4 * 512, BF16)
        sb_h24 = arena.sub("accpad", 0, 1024, F32)
        sb_h24b = arena.sub("acc", 0, 1024, BF16)
        sb_h24c = arena.sub("acc", 1024, 1024, F32)
        sb_x3hT = arena.alloc("x3hT", 8 * 4, BF16)

        evh = P.dma("sync", sb_h24[0:24, :], cc_halo_out.ap(), after=[evcc])
        evhb = vcopy(sb_h24b[0:24, :], sb_h24[0:24, :], after=[evh])
        hsel = []
        for mo in range(2):
            hsel.append(mm(psA[0:3, mo * 512:(mo + 1) * 512],
                           sb_halosel[0:24, 0:3],
                           sb_h24b[0:24, mo * 512:(mo + 1) * 512],
                           start=True, stop=True, after=[evhb]))
        evx2h = scopy(sb_h24[0:3, :], psA[0:3, 0:1024], after=hsel)
        evlnh = ln_tile(sb_h24[0:3, :], sb_h24c[0:3, :], sb_stat, after=[evx2h])
        hts = []
        for fc in range(8):
            et = tpose(psB[:, 1536:1539], sb_h24c[0:3, fc * 128:(fc + 1) * 128],
                       after=[evlnh] + hts[-1:])
            ee = scopy(sb_x3hT[:, fc * 4:fc * 4 + 3], psB[:, 1536:1539], after=[et])
            hts.append(ee)

        carry_evs = {}
        winr_user = [None, None, None]
        ln3_user = {0: None, 1: None}
        prev5 = [evA] + hts
        prev5w = []
        for chk in range(4):
            tp_evs = []
            for j in range(4):
                tile = chk * 4 + j
                rb = j % 2
                lt = sb_ln3[:, rb * 1024:(rb + 1) * 1024]
                dep = ([ln3_user[rb]] if ln3_user[rb] else []) + prev5
                evn = ln_tile(sb_stream[:, tile * 1024:(tile + 1) * 1024], lt,
                              sb_stat, after=dep)
                evs = tp_tile_to_fmajor(lt, sb_x3T, lambda fc: fc * 512, j, [evn])
                ln3_user[rb] = evs[-1]
                tp_evs += evs
                P.mbar()
            prev5 = []

            psB_user = tp_evs
            psH_user = hts
            evmu = []
            zsil = []
            for zc in range(32):
                rslot = zc % 3
                dep = [winr_user[rslot]] if winr_user[rslot] else []
                evw = P.dma("gpsimd",
                            sb_Winr[:, rslot * 1024:(rslot + 1) * 1024]
                            .rearrange("p (c f) -> p c f", f=128),
                            Win_h[:, zc * 128:(zc + 1) * 128]
                            .rearrange("(c p) f -> p c f", p=128),
                            after=dep)
                mmevs = []
                for ifc in range(8):
                    aft = []
                    if ifc == 0:
                        aft = tp_evs + [evw] + psB_user
                    mmevs.append(mm(psB[:, 0:512],
                                    sb_Winr[:, rslot * 1024 + ifc * 128:
                                            rslot * 1024 + ifc * 128 + 128],
                                    sb_x3T[:, ifc * 512:(ifc + 1) * 512],
                                    start=(ifc == 0), stop=(ifc == 7), after=aft))
                winr_user[rslot] = mmevs[-1]
                if zc < 16:
                    if chk == 0:
                        hmm = []
                        for ifc in range(8):
                            hmm.append(mm(psB[:, 512:515],
                                          sb_Winr[:, rslot * 1024 + ifc * 128:
                                                  rslot * 1024 + ifc * 128 + 128],
                                          sb_x3hT[:, ifc * 4:ifc * 4 + 3],
                                          start=(ifc == 0), stop=(ifc == 7),
                                          after=psH_user if ifc == 0 else []))
                        winr_user[rslot] = hmm[-1]
                        evcar = scopy(sb_carry[:, zc * 4:zc * 4 + 3],
                                      psB[:, 512:515], after=[hmm[-1]])
                        carry_evs[zc] = evcar
                        psH_user = [evcar]
                    evc0 = vcopy(sb_xcp[:, 0:3], sb_carry[:, zc * 4:zc * 4 + 3],
                                 after=[carry_evs[zc]])
                    evc1 = scopy(sb_xcp[:, 3:515], psB[:, 0:512],
                                 after=[mmevs[-1], carry_evs[zc]])
                    psB_user = [evc1]
                    evcar = vcopy(sb_carry[:, zc * 4:zc * 4 + 3],
                                  sb_xcp[:, 512:515], after=[evc1, evc0])
                    carry_evs[zc] = evcar
                    evcv = vts(sb_csc, sb_xcp[:, 0:512],
                               sb_convw[:, zc * 4:zc * 4 + 1], None, ALU.mult,
                               after=[evc0, evc1])
                    for jt in range(1, 4):
                        evcv = vstt(sb_csc, sb_xcp[:, jt:jt + 512],
                                    sb_convw[:, zc * 4 + jt:zc * 4 + jt + 1],
                                    sb_csc, ALU.mult, ALU.add, after=[evcv])
                    evsl = act("scalar", sb_xcn[:, zc * 512:(zc + 1) * 512], sb_csc,
                               AF.Silu, after=[evcv])
                    evsq = act("scalar", sb_sq, sb_xcn[:, zc * 512:(zc + 1) * 512],
                               AF.Square, after=[evsl])
                    evmu.append(mm(psA[0:1, 1024:1536], sb_lnsc,
                                   sb_xcn[:, zc * 512:(zc + 1) * 512],
                                   start=(zc == 0), stop=(zc == 15), after=[evsl]))
                    evmu.append(mm(psA[0:1, 512:1024], sb_lnsc, sb_sq,
                                   start=(zc == 0), stop=(zc == 15), after=[evsq]))
                else:
                    ev = act("scalar", sb_zs[:, (zc - 16) * 512:(zc - 15) * 512],
                             psB[:, 0:512], AF.Silu, after=[mmevs[-1]])
                    psB_user = [ev]
                    zsil.append(ev)
                P.mbar()
            evm = scopy(sb_mrow[0:1, :], psA[0:1, 1024:1536], after=evmu)
            evv0 = scopy(sb_vrow[0:1, :], psA[0:1, 512:1024], after=evmu)
            evm2 = vtt(sb_rrow[0:1, :], sb_mrow[0:1, :], sb_mrow[0:1, :],
                       ALU.mult, after=[evm])
            evvar = vtt(sb_vrow[0:1, :], sb_vrow[0:1, :], sb_rrow[0:1, :],
                        ALU.subtract, after=[evv0, evm2])
            evr1 = act("scalar", sb_rrow[0:1, :], sb_vrow[0:1, :], AF.Ln,
                       bias=sb_eps[0:1, 0:1], after=[evvar])
            evr2 = act("scalar", sb_rrow[0:1, :], sb_rrow[0:1, :], AF.Exp,
                       scale=-0.5, after=[evr1])
            evbm = vcopy(sb_rowb[0:1, 0:512], sb_mrow[0:1, :], after=[evm2])
            evbr = vcopy(sb_rowb[0:1, 512:1024], sb_rrow[0:1, :], after=[evr2])
            evb1 = mm(psA[:, 1024:1536], sb_ones1b[0:1, :], sb_rowb[0:1, 0:512],
                      start=True, stop=True, after=[evbm] + evmu)
            evb2 = mm(psA[:, 1536:2048], sb_ones1b[0:1, :], sb_rowb[0:1, 512:1024],
                      start=True, stop=True, after=[evbr] + [tp_user[0], tp_user[1]])
            evn_l = []
            for zc in range(16):
                blk = sb_xcn[:, zc * 512:(zc + 1) * 512]
                e1 = vstt(sb_csc, psA[:, 1024:1536], -1.0, blk, ALU.mult, ALU.add,
                          after=[evb1, evb2] + evn_l[-1:])
                e2 = vtt(sb_csc, sb_csc, psA[:, 1536:2048], ALU.mult, after=[e1])
                e3 = act("scalar", blk, sb_csc, AF.Identity,
                         bias=sb_lnbeta[:, zc:zc + 1],
                         scale=sb_lng[:, zc:zc + 1], after=[e2])
                evn_l.append(e3)
                P.mbar()
            xpe = []
            for zc in range(16):
                xpe.append(mm(psB[0:33, 0:512], sb_Wxp[:, zc * 33:(zc + 1) * 33],
                              sb_xcn[:, zc * 512:(zc + 1) * 512],
                              start=(zc == 0), stop=(zc == 15),
                              after=(evn_l + psB_user) if zc == 0 else [evn_l[zc]]))
            evxp = scopy(sb_xph[0:33, :], psB[0:33, 0:512], after=[xpe[-1]])
            psB_user = [evxp]
            evd = softplus(sb_mrow[0:1, :], sb_xph[0:1, :], sb_vrow[0:1, :],
                           after=[evxp, evb1])
            evd = vts(sb_mrow[0:1, :], sb_mrow[0:1, :], 1e-6, 10.0,
                      ALU.max, ALU.min, after=[evd])
            evdb = vcopy(sb_rowb[0:1, 0:512], sb_mrow[0:1, :], after=[evd, evb1])
            evdb2 = mm(psA[:, 1024:1536], sb_ones1b[0:1, :], sb_rowb[0:1, 0:512],
                       start=True, stop=True, after=[evdb] + evn_l)
            evdbc = scopy(sb_dbc, psA[:, 1024:1536], after=[evdb2])
            evs1 = P.dma("sync", sb_bcf[0:16, 0:512], sb_xph[1:17, :],
                         after=[evxp])
            evs2 = P.dma("sync", sb_bcf[0:16, 512:1024], sb_xph[17:33, :],
                         after=[evxp])
            evbcb = vtt(sb_bcb[0:16, :], sb_bcf[0:16, 0:512],
                        sb_bcf[0:16, 512:1024], ALU.mult, after=[evs1, evs2])
            accw = {}
            slot_user = [evn_l, evn_l, evn_l, evn_l]
            round_end = []
            for n in range(16):
                evn0 = mm(psB[:, 1536:2048], sb_sel16[0:16, n * 128:(n + 1) * 128],
                          sb_bcb[0:16, :], start=True, stop=True,
                          after=[evbcb, evdbc] + round_end)
                evcp = vcopy(sb_bcbs, psB[:, 1536:2048], after=[evn0])
                round_end = [evcp]
                for ic in range(16):
                    sl = ic % 4
                    evx = act("scalar", sb_csc4[:, sl * 512:(sl + 1) * 512],
                              sb_dbc, AF.Exp,
                              scale=sb_A[:, ic * 16 + n:ic * 16 + n + 1],
                              after=([evdbc] if n == 0 and ic < 4 else [])
                              + slot_user[sl])
                    evt = vstt(sb_term4[:, sl * 512:(sl + 1) * 512],
                               sb_csc4[:, sl * 512:(sl + 1) * 512], 1e-6,
                               sb_bcbs, ALU.max, ALU.mult, after=[evx, evcp])
                    if n == 0:
                        accw[ic] = vcopy(sb_acc[:, ic * 512:(ic + 1) * 512],
                                         sb_term4[:, sl * 512:(sl + 1) * 512],
                                         after=[evt])
                    else:
                        accw[ic] = vadd(sb_acc[:, ic * 512:(ic + 1) * 512],
                                        sb_acc[:, ic * 512:(ic + 1) * 512],
                                        sb_term4[:, sl * 512:(sl + 1) * 512],
                                        after=[evt, accw[ic]])
                    slot_user[sl] = [accw[ic]]
                P.mbar()
            if chk == 0 and "p5dbg" in dbg:
                ev = P.dma("sync", dout("xcn0", (128, 16 * 512), BF16), sb_xcn,
                           after=evn_l + [evxp])
                prev5.append(ev)
                ev = P.dma("sync", dout("acc0", (128, 16 * 512), BF16), sb_acc,
                           after=[accw[ic] for ic in range(16)])
                prev5.append(ev)
                ev = P.dma("sync", dout("xph0", (33, 512)), sb_xph[0:33, :],
                           after=[evxp, evd, evbcb])
                prev5.append(ev)
                ev = P.dma("sync", dout("zs0", (128, 16 * 512), BF16), sb_zs,
                           after=zsil)
                prev5.append(ev)
            ye = []
            for ic in range(16):
                e1 = vstt(sb_y[:, ic * 512:(ic + 1) * 512],
                          sb_acc[:, ic * 512:(ic + 1) * 512],
                          sb_Dv[:, ic:ic + 1],
                          sb_xcn[:, ic * 512:(ic + 1) * 512],
                          ALU.add, ALU.mult, after=[accw[ic]])
                e2 = vtt(sb_y[:, ic * 512:(ic + 1) * 512],
                         sb_y[:, ic * 512:(ic + 1) * 512],
                         sb_zs[:, ic * 512:(ic + 1) * 512], ALU.mult,
                         after=[e1] + zsil)
                ye.append(e2)
            pso = [psA[:, 0:1024], psA[:, 1024:2048],
                   psB[:, 0:1024], psB[:, 1024:2048]]
            wor_user = [None, None, None]
            mmlast = [None] * 4
            for ic in range(16):
                rs = ic % 3
                dep = [wor_user[rs]] if wor_user[rs] else []
                evw = P.dma("gpsimd", sb_Woutr[:, rs * 1024:(rs + 1) * 1024],
                            Wout_h[ic * 128:(ic + 1) * 128, :], after=dep)
                for ts_ in range(4):
                    for mo in range(2):
                        aft = [evw]
                        if ic == 0 and ts_ == 0 and mo == 0:
                            aft = aft + ye + psB_user + [evdbc] + prev5w
                        mmlast[ts_] = mm(pso[ts_][:, mo * 512:(mo + 1) * 512],
                                         sb_y[:, ic * 512 + ts_ * 128:
                                              ic * 512 + ts_ * 128 + 128],
                                         sb_Woutr[:, rs * 1024 + mo * 512:
                                                  rs * 1024 + mo * 512 + 512],
                                         start=(ic == 0), stop=(ic == 15),
                                         after=aft if mo == 0 and ts_ == 0 else [])
                wor_user[rs] = mmlast[3]
            prev5 = []
            for ts_ in range(4):
                tile = chk * 4 + ts_
                ev1 = vadd(sb_stream[:, tile * 1024:(tile + 1) * 1024],
                           pso[ts_], sb_stream[:, tile * 1024:(tile + 1) * 1024],
                           after=[mmlast[ts_]])
                prev5.append(ev1)
            prev5w = prev5
            P.mbar()

        P.barrier()
        for nm in ("Winr", "Woutr", "x3T", "xcn", "zs", "acc", "y",
                   "carry", "xcp", "csc", "sq", "xph", "mrow", "vrow", "rrow",
                   "rowb", "dbc", "bcf", "bcb", "bcbs", "term", "accpad", "x3hT",
                   "A", "convw", "lng", "lnbeta", "Dv", "Wxp", "halosel"):
            arena.free(nm)

        if "x3" in dbg:
            o = dout("x3", (T, D))
            for t in range(NT):
                P.dma("sync", o[t * 128:(t + 1) * 128, :],
                      sb_stream[:, t * 1024:(t + 1) * 1024])
        if phase_limit <= 5:
            P.barrier(); P.emit(block)
            return nc, dbg_outs, P

        sb_xn4 = arena.alloc("xn4", 16 * 1024, BF16)
        sb_lg = arena.alloc("lg", 2 * 1024, F32)
        sb_lgT = arena.alloc("lgT", 1024, F32)
        sb_logit = arena.alloc("logit", 16 * 8, F32)
        sb_esc = arena.alloc("esc", 8, F32)
        sb_g1 = arena.alloc("g1", 16, F32)
        sb_mt = arena.alloc("mt", 8, F32)
        sb_stat2 = arena.alloc("stat2", 8, F32)
        sb_mem = arena.alloc("mem", 2048, F32)
        sb_incl = arena.alloc("incl", 2048, F32)
        sb_lfr = arena.alloc("lfr", 2048, F32)
        sb_rowpair = arena.alloc("rowpair", 2 * 2048, F32)
        sb_lfrow = sb_rowpair[:, 0:2048]
        kept_row = sb_rowpair[0:1, 2048:4096]
        sb_lfc = arena.alloc("lfc", 16, F32)
        sb_kg = arena.alloc("kg", 16, F32)

        ln4_user = {0: None, 1: None}
        mem_evs = []
        lg_user = {0: None, 1: None}
        mm8_user = []
        mt_user = []
        for t in range(NT):
            rb = t % 2
            xt = sb_stream[:, t * 1024:(t + 1) * 1024]
            dep = [ln4_user[rb]] if ln4_user[rb] else []
            evn = ln_tile(xt, sb_xn4[:, t * 1024:(t + 1) * 1024], sb_stat,
                          after=dep)
            ln4_user[rb] = evn
            lgt = sb_lg[:, rb * 1024:(rb + 1) * 1024]
            dep = [lg_user[rb]] if lg_user[rb] else []
            evf = ln_tile(xt, lgt, sb_stat, after=[evn] + dep)
            mme = []
            lastev = [evf]
            for fc in range(8):
                sl = fc % 2
                pslice = psA[:, 1024 + sl * 128:1024 + sl * 128 + 128]
                et = tpose(pslice, lgt[:, fc * 128:(fc + 1) * 128],
                           after=lastev, f32=True)
                ee = scopy(sb_lgT[:, fc * 128:(fc + 1) * 128], pslice, after=[et])
                lastev = [ee]
                mme.append(mm(psB[:, 0:8], sb_lgT[:, fc * 128:(fc + 1) * 128],
                              sb_wg[:, fc * 8:(fc + 1) * 8],
                              start=(fc == 0), stop=(fc == 7),
                              after=([ee] + mm8_user) if fc == 0 else [ee]))
            lg_user[rb] = lastev[0]
            evlg = scopy(sb_logit[:, t * 8:(t + 1) * 8], psB[:, 0:8],
                         after=[mme[-1]])
            mm8_user = [evlg]
            lcol = sb_logit[:, t * 8:(t + 1) * 8]
            evmx = P.op("vector", lambda e, lcol=lcol: e.reduce_max(
                sb_stat2[:, 0:1], lcol, axis=AX.X), after=[evlg] + mt_user)
            evmk = vts(sb_mt, lcol, sb_stat2[:, 0:1], None, ALU.is_equal,
                       after=[evmx])
            evnm = vts(sb_stat2[:, 1:2], sb_stat2[:, 0:1], -1.0, None, ALU.mult,
                       after=[evmx])
            evex = act("scalar", sb_esc, lcol, AF.Exp, bias=sb_stat2[:, 1:2],
                       accum=sb_stat2[:, 2:3], after=[evnm])
            evrc = P.op("vector", lambda e, t=t: e.reciprocal(
                sb_g1[:, t:t + 1], sb_stat2[:, 2:3]), after=[evex])
            et = tpose(psA[0:8, 1280:1408], sb_mt, after=[evmk], f32=True)
            ee = scopy(sb_mem[0:8, t * 128:(t + 1) * 128], psA[0:8, 1280:1408],
                       after=[et])
            mt_user = [ee, evrc]
            mem_evs.append(ee)
            mem_evs.append(evrc)
            P.mbar()

        if "p6dbg" in dbg:
            P.dma("sync", dout("lgt", (128, 128)), sb_logit, after=mem_evs)
            P.dma("sync", dout("g1d", (128, 16)), sb_g1, after=mem_evs)
            P.dma("sync", dout("memd", (8, 2048)), sb_mem[0:8, :], after=mem_evs)
        evscan = P.op("vector", lambda e: e.tensor_tensor_scan(
            sb_incl[0:8, :], sb_mem[0:8, :], sb_mem[0:8, :], 0.0,
            op0=ALU.add, op1=ALU.bypass), after=mem_evs)
        evcnt = vcopy(sb_lfr[0:8, 0:1], sb_incl[0:8, 2047:2048], after=[evscan])
        evcd = P.dma("sync", cc_cnt_in.ap(), sb_lfr[0:8, 0:1], after=[evcnt])
        evcc2 = P.op("gpsimd", lambda e: e.collective_compute(
            "AllGather", ALU.bypass, ins=[cc_cnt_in.ap()],
            outs=[cc_cnt_out.ap()], replica_groups=[[0, 1, 2, 3, 4, 5, 6, 7]]),
            after=[evcd])

        evlf1 = vts(sb_lfr[0:8, :], sb_incl[0:8, :], sb_ebase[0:8, :], None,
                    ALU.add, after=[evscan, evcd])
        evlf2 = vtt(sb_lfr[0:8, :], sb_lfr[0:8, :], sb_mem[0:8, :], ALU.mult,
                    after=[evlf1])
        lfm = []
        for mo in range(4):
            lfm.append(mm(psA[0:1, mo * 512:(mo + 1) * 512], sb_ones8[0:8, :],
                          sb_lfr[0:8, mo * 512:(mo + 1) * 512],
                          start=True, stop=True,
                          after=[evlf2] + ([] if mo else mm8_user)))
        evlfr = scopy(sb_lfrow[0:1, :], psA[0:1, 0:2048], after=lfm)
        lfc_evs = []
        for t in range(NT):
            sl = t % 2
            pslice = psA[:, 1024 + sl * 128:1024 + sl * 128 + 1]
            et = tpose(pslice, sb_lfrow[0:1, t * 128:(t + 1) * 128],
                       after=[evlfr] + lfc_evs[-2:-1], f32=True)
            ee = scopy(sb_lfc[:, t:t + 1], pslice, after=[et])
            lfc_evs.append(ee)

        if "p6dbg" in dbg:
            P.dma("sync", dout("incld", (8, 2048)), sb_incl[0:8, :],
                  after=[evscan, evlf2])
            P.dma("sync", dout("lfrowd", (1, 2048)), sb_lfrow[0:1, :],
                  after=[evlfr])
        sb_c8 = arena.alloc("c8", 8, F32)
        evagd = P.dma("sync", sb_c8[0:8, 0:8],
                      cc_cnt_out.ap().rearrange("(r e) o -> r (e o)", e=8),
                      after=[evcc2])
        evoff = mm(psA[0:1, 0:8], sb_ranksel[0:8, :], sb_c8[0:8, 0:8],
                   start=True, stop=True, after=[evagd] + lfc_evs)
        evoffs = scopy(sb_c8[0:1, 0:8], psA[0:1, 0:8], after=[evoff])
        evofc = tpose(psA[0:8, 8:9], sb_c8[0:1, 0:8], after=[evoffs], f32=True)
        evofc2 = scopy(sb_stat2[0:8, 4:5], psA[0:8, 8:9], after=[evofc])
        evk1 = vts(sb_lfr[0:8, :], sb_incl[0:8, :], sb_stat2[0:8, 4:5], None,
                   ALU.add, after=[evofc2, evlf2])
        evk2 = vts(sb_lfr[0:8, :], sb_lfr[0:8, :], 2048.5, None, ALU.is_le,
                   after=[evk1])
        evk3 = vtt(sb_lfr[0:8, :], sb_lfr[0:8, :], sb_mem[0:8, :], ALU.mult,
                   after=[evk2])
        kr = []
        for mo in range(4):
            kr.append(mm(psA[0:1, mo * 512:(mo + 1) * 512], sb_ones8[0:8, :],
                         sb_lfr[0:8, mo * 512:(mo + 1) * 512],
                         start=True, stop=True, after=[evk3]))
        evkr = scopy(kept_row, psA[0:1, 0:2048], after=kr)
        kg_evs = []
        for t in range(NT):
            et = tpose(psA[:, 8:9], kept_row[:, t * 128:(t + 1) * 128],
                       after=[evkr] + kg_evs[-1:], f32=True)
            ee = scopy(sb_stat2[:, 5:6], psA[:, 8:9], after=[et])
            ev = vtt(sb_kg[:, t:t + 1], sb_stat2[:, 5:6], sb_g1[:, t:t + 1],
                     ALU.mult, after=[ee])
            kg_evs.append(ev)

        arena.free("lg"); arena.free("lgT"); arena.free("wg")
        sb_Dm = arena.alloc("Dm", 8 * 512, BF16)
        disp_e = [arena.alloc(f"disp{e}", 8 * CAPL, BF16) for e in range(5)]
        disp_e.append(arena.sub("mem", 0, 8 * CAPL, BF16))
        disp_e.append(arena.sub("incl", 0, 8 * CAPL, BF16))
        disp_e.append(arena.sub("lfr", 0, 8 * CAPL, BF16))

        def psd(mc):
            return (psA if mc < 4 else psB)[:, (mc % 4) * 512:(mc % 4) * 512 + 512]

        disp_ev = []
        dstat_user = []
        evac_ev = []
        dm_guard = []
        for sc in range(SLOTS // 512):
            mm_half = []
            for half in range(2):
                de = []
                for tt in range(8):
                    t = half * 8 + tt
                    e1 = vts(sb_stat2[:, 3:4], sb_lfc[:, t:t + 1],
                             -(sc * 512.0), None, ALU.add,
                             after=[lfc_evs[t], evkr] + kg_evs[-1:] + dstat_user)
                    e2 = vts(sb_Dm[:, tt * 512:(tt + 1) * 512], sb_iota512,
                             sb_stat2[:, 3:4], None, ALU.is_equal,
                             after=[e1] + mm_half[-1:] + dm_guard)
                    dstat_user = [e2]
                    de.append(e2)
                for mc in range(8):
                    for tt in range(8):
                        t = half * 8 + tt
                        aft = []
                        if tt == 0:
                            aft = de + (evac_ev + lfc_evs
                                        if (half == 0 and mc == 0) else [])
                        ev = mm(psd(mc),
                                sb_xn4[:, t * 1024 + mc * 128:
                                       t * 1024 + mc * 128 + 128],
                                sb_Dm[:, tt * 512:(tt + 1) * 512],
                                start=(t == 0), stop=(t == 15), after=aft)
                    mm_half.append(ev)
            dm_guard = mm_half[-1:]
            evac_ev = []
            for mc in range(8):
                lo = sc * 512
                while lo < (sc + 1) * 512:
                    e_ = lo // CAPL
                    hi = min((sc + 1) * 512, (e_ + 1) * CAPL)
                    ev = scopy(disp_e[e_][:, mc * CAPL + (lo - e_ * CAPL):
                                          mc * CAPL + (hi - e_ * CAPL)],
                               psd(mc)[:, (lo - sc * 512):(hi - sc * 512)],
                               after=mm_half[-1:])
                    disp_ev.append(ev)
                    evac_ev.append(ev)
                    lo = hi
            P.mbar()
        P.barrier()
        arena.free("xn4"); arena.free("Dm")

        if "p6dbg" in dbg:
            ev = P.dma("sync", dout("disp0", (128, 8 * 384), BF16), disp_e[0],
                       after=disp_ev)
            disp_ev = disp_ev + [ev]
        sb_h = arena.alloc("h", 32 * CAPL, BF16)
        sb_w1r = arena.alloc("w1r", 3 * 1024, BF16)
        sb_w2r = arena.alloc("w2r", 3 * 1024, BF16)
        sb_b2bc = arena.alloc("b2bc", 1024, BF16)
        eo_e = [None] * EXPERTS
        w1_user = [None, None, None]
        w2_user = [None, None, None]
        psb384_user = disp_ev
        eo_evs = []
        for e_ in range(EXPERTS):
            hev = []
            for fc in range(32):
                rs = fc % 3
                dep = [w1_user[rs]] if w1_user[rs] else []
                evw = P.dma("gpsimd",
                            sb_w1r[:, rs * 1024:(rs + 1) * 1024]
                            .rearrange("p (c f) -> p c f", f=128),
                            w1_h[e_][:, fc * 128:(fc + 1) * 128]
                            .rearrange("(c p) f -> p c f", p=128),
                            after=dep)
                mmv = []
                for mc in range(8):
                    aft = [evw]
                    if mc == 0:
                        aft = aft + psb384_user
                    mmv.append(mm(psB[:, 0:CAPL],
                                  sb_w1r[:, rs * 1024 + mc * 128:
                                         rs * 1024 + mc * 128 + 128],
                                  disp_e[e_][:, mc * CAPL:(mc + 1) * CAPL],
                                  start=(mc == 0), stop=(mc == 7),
                                  after=aft if mc == 0 else []))
                w1_user[rs] = mmv[-1]
                ev = act("scalar", sb_h[:, fc * CAPL:(fc + 1) * CAPL],
                         psB[:, 0:CAPL], AF.Gelu_apprx_tanh,
                         bias=sb_b1[:, e_ * 32 + fc:e_ * 32 + fc + 1],
                         after=[mmv[-1]])
                psb384_user = [ev]
                hev.append(ev)
                P.mbar()
            if e_ < 5:
                arena.free(f"disp{e_}")
            if e_ == 5:
                eo_e[e_] = arena.sub("mem", 0, (CAPL // 128) * 1024, BF16)
            elif e_ == 6:
                eo_e[e_] = arena.sub("incl", 0, (CAPL // 128) * 1024, BF16)
            elif e_ == 7:
                eo_e[e_] = arena.sub("lfr", 0, (CAPL // 128) * 1024, BF16)
            else:
                eo_e[e_] = arena.alloc(f"eo{e_}", (CAPL // 128) * 1024, BF16)
            evb2m = None
            for mo in range(2):
                evb2m = mm(psA[:, mo * 512:(mo + 1) * 512],
                           sb_sel16[0:8, e_ * 128:(e_ + 1) * 128],
                           sb_b2[0:8, mo * 512:(mo + 1) * 512],
                           start=True, stop=True,
                           after=(eo_evs[-1:] + hev[-1:]) if mo == 0 else [])
            evb2 = scopy(sb_b2bc, psA[:, 0:1024], after=[evb2m])
            pse = [psB[:, 1024:2048], psA[:, 1024:2048], psB[:, 0:1024],
                   psA[:, 0:1024]]
            mm_last = [None] * (CAPL // 128)
            for fc in range(32):
                rs = fc % 3
                dep = [w2_user[rs]] if w2_user[rs] else []
                evw = P.dma("gpsimd", sb_w2r[:, rs * 1024:(rs + 1) * 1024],
                            w2_h[e_][fc * 128:(fc + 1) * 128, :], after=dep)
                for sc3 in range(CAPL // 128):
                    for mo in range(2):
                        aft = [evw]
                        if fc == 0 and mo == 0:
                            aft = aft + hev + [evb2] + eo_evs[-4:]
                        mm_last[sc3] = mm(pse[sc3][:, mo * 512:(mo + 1) * 512],
                                          sb_h[:, fc * CAPL + sc3 * 128:
                                               fc * CAPL + sc3 * 128 + 128],
                                          sb_w2r[:, rs * 1024 + mo * 512:
                                                 rs * 1024 + mo * 512 + 512],
                                          start=(fc == 0), stop=(fc == 31),
                                          after=aft if (sc3 == 0 and mo == 0) else [])
                w2_user[rs] = mm_last[-1]
            for sc3 in range(CAPL // 128):
                ev = vtt(eo_e[e_][:, sc3 * 1024:(sc3 + 1) * 1024],
                         pse[sc3][:, 0:1024], sb_b2bc, ALU.add,
                         after=[mm_last[-1], evb2])
                eo_evs.append(ev)
            psb384_user = eo_evs[-4:]
            P.mbar()
        P.barrier()
        arena.free("w1r"); arena.free("w2r")
        arena.free("b2bc")

        if "p6dbg" in dbg:
            P.dma("sync", dout("eo0", (128, 3 * 1024), BF16), eo_e[0],
                  after=eo_evs[0:3])

        if "p6dbg" in dbg:
            P.dma("sync", dout("kgd", (128, 16)), sb_kg, after=kg_evs)
            P.dma("sync", dout("offsd", (8, 1)), sb_stat2[0:8, 4:5],
                  after=[evofc2, evk1])
        lfb = []
        for mo in range(4):
            lfb.append(mm(psA[:, mo * 512:(mo + 1) * 512], sb_ones1[0:1, :],
                          sb_lfrow[0:1, mo * 512:(mo + 1) * 512],
                          start=True, stop=True, after=[evkr] + kg_evs))

        sb_Cp = arena.alloc("Cp", 2 * 128, BF16)
        cuser = [None, None]
        out_evs = []
        for t in range(NT):
            mm_last = None
            for sc in range(SLOTS // 128):
                rb = sc % 2
                dep = [cuser[rb]] if cuser[rb] else []
                evc = vts(sb_Cp[:, rb * 128:(rb + 1) * 128],
                          psA[:, t * 128:(t + 1) * 128],
                          sb_slotid[:, sc:sc + 1], None, ALU.is_equal,
                          after=lfb + dep)
                for mo in range(2):
                    aft = [evc]
                    if sc == 0 and mo == 0:
                        aft = aft + eo_evs + out_evs[-1:]
                    mm_last = mm(psB[:, mo * 512:(mo + 1) * 512],
                                 sb_Cp[:, rb * 128:(rb + 1) * 128],
                                 eo_e[sc // (CAPL // 128)]
                                 [:, (sc % (CAPL // 128)) * 1024 + mo * 512:
                                  (sc % (CAPL // 128)) * 1024 + mo * 512 + 512],
                                 start=(sc == 0), stop=(sc == SLOTS // 128 - 1),
                                 after=aft if mo == 0 else [])
                cuser[rb] = mm_last
            evo = vstt(sb_stream[:, t * 1024:(t + 1) * 1024], psB[:, 0:1024],
                       sb_kg[:, t:t + 1],
                       sb_stream[:, t * 1024:(t + 1) * 1024],
                       ALU.mult, ALU.add, after=[mm_last] + kg_evs)
            out_evs.append(evo)
            P.dma("sync", out_h[t * 128:(t + 1) * 128, :],
                  sb_stream[:, t * 1024:(t + 1) * 1024], after=[evo])
            P.mbar()

        P.barrier()
        P.emit(block)
    return nc, dbg_outs, P


_CACHE = {}


def _get_program(dbg=(), phase_limit=99, auto_bar=False):
    key = (tuple(sorted(dbg)), phase_limit, auto_bar)
    if key not in _CACHE:
        _CACHE[key] = build_program(dbg=dbg, phase_limit=phase_limit,
                                    auto_bar=auto_bar)
    return _CACHE[key]


def make_in_maps(inputs):
    X = np.ascontiguousarray(np.asarray(inputs["L0_attn_x"], dtype=np.float32))
    base = dict(
        Mq=np.ascontiguousarray(inputs["L0_attn_Mq"], dtype=np.float32),
        Mk=np.ascontiguousarray(inputs["L0_attn_Mk"], dtype=np.float32),
        Mv=np.ascontiguousarray(inputs["L0_attn_Mv"], dtype=np.float32),
        G1R=np.ascontiguousarray(np.asarray(inputs["L0_ffn_G1_R"],
                                            dtype=np.float32).reshape(32, 4, 32)),
        G2R=np.ascontiguousarray(np.asarray(inputs["L0_ffn_G2_R"],
                                            dtype=np.float32).reshape(4, 16, 16)),
        G1I=np.ascontiguousarray(np.asarray(inputs["L0_ffn_G1_I"],
                                            dtype=np.float32).reshape(32, 4, 32)),
        G2I=np.ascontiguousarray(np.asarray(inputs["L0_ffn_G2_I"],
                                            dtype=np.float32).reshape(4, 16, 16)),
        bR=np.ascontiguousarray(inputs["L0_ffn_bias_real"], dtype=np.float32),
        bI=np.ascontiguousarray(inputs["L0_ffn_bias_imag"], dtype=np.float32),
        W_in=np.ascontiguousarray(inputs["L1_attn_W_in"], dtype=np.float32),
        conv_w=np.ascontiguousarray(np.asarray(inputs["L1_attn_conv_w"],
                                               dtype=np.float32).reshape(2048, 4)),
        ln_g=np.ascontiguousarray(inputs["L1_attn_ln_gamma"], dtype=np.float32),
        ln_b=np.ascontiguousarray(inputs["L1_attn_ln_beta"], dtype=np.float32),
        W_xp=np.ascontiguousarray(inputs["L1_attn_W_xproj"], dtype=np.float32),
        A_p=np.ascontiguousarray(inputs["L1_attn_A_param"], dtype=np.float32),
        Dvec=np.ascontiguousarray(inputs["L1_attn_D"], dtype=np.float32),
        W_out=np.ascontiguousarray(inputs["L1_attn_W_out"], dtype=np.float32),
        wg=np.ascontiguousarray(inputs["L1_ffn_wg"], dtype=np.float32),
        w1=np.ascontiguousarray(inputs["L1_ffn_w1"], dtype=np.float32),
        b1=np.ascontiguousarray(inputs["L1_ffn_b1"], dtype=np.float32),
        w2=np.ascontiguousarray(inputs["L1_ffn_w2"], dtype=np.float32),
        b2=np.ascontiguousarray(inputs["L1_ffn_b2"], dtype=np.float32),
    )
    in_maps = []
    for core in range(8):
        b, half = core // 2, core % 2
        m = dict(base)
        m["x_full"] = np.ascontiguousarray(X[b])
        m["x_own"] = np.ascontiguousarray(X[b][half * T:(half + 1) * T])
        m["rank_sel"] = (np.arange(8) < core).astype(np.float32).reshape(8, 1)
        hs = np.zeros((24, 3), dtype=np.float32)
        if half == 1:
            partner = core - 1
            for j in range(3):
                hs[partner * 3 + j, j] = 1.0
        m["halo_sel24"] = hs
        in_maps.append(m)
    return in_maps


def kernel(**inputs):
    nc, dbg_outs, P = _get_program()
    in_maps = make_in_maps(inputs)
    res = run_bass_kernel_spmd(nc, in_maps, core_ids=list(range(8)))
    out = np.zeros((4, S, D), dtype=np.float32)
    for core in range(8):
        b, half = core // 2, core % 2
        out[b, half * T:(half + 1) * T] = np.asarray(res.results[core]["out"])
    return out
